# version 19
# baseline (speedup 1.0000x reference)
import numpy as np
import ml_dtypes
import concourse.bass as bass
import concourse.mybir as mybir
from concourse.bass_utils import run_bass_kernel_spmd

F32 = mybir.dt.float32
BF16 = mybir.dt.bfloat16
AX = mybir.AxisListType
ALU = mybir.AluOpType
ACT = mybir.ActivationFunctionType

H = 1024
M = 256
B = 16
TILE = 128
HC = H // 128
MC = M // 128
DECAY = 0.99
EPS = 1e-5


def build_nc(NT=32, NCORES=8):
    NG = NT // 4
    nc = bass.Bass("TRN2", target_bir_lowering=False, debug=False,
                   dynamic_dma_scratch_size=512)

    x_d = nc.dram_tensor("x", [NT * TILE, H], F32, kind="ExternalInput").ap()
    wt_d = nc.dram_tensor("wt", [H, H], F32, kind="ExternalInput").ap()
    wqkt_d = nc.dram_tensor("wqkt", [H, 2 * M], F32, kind="ExternalInput").ap()
    wvt_d = nc.dram_tensor("wvt", [H, M], F32, kind="ExternalInput").ap()
    wot_d = nc.dram_tensor("wot", [M, H], F32, kind="ExternalInput").ap()
    m0t_d = nc.dram_tensor("m0t", [M, M], F32, kind="ExternalInput").ap()
    mask_d = nc.dram_tensor("mask", [128, 128], F32, kind="ExternalInput").ap()
    idb_d = nc.dram_tensor("idb", [128, 128], BF16, kind="ExternalInput").ap()
    smalls_d = nc.dram_tensor("smalls", [1, 128], F32, kind="ExternalInput").ap()
    y_d = nc.dram_tensor("y", [NT * TILE, H], F32, kind="ExternalOutput").ap()
    cc_in = nc.dram_tensor("cc_in", [M, M], F32)
    cc_out = nc.dram_tensor("cc_out", [NCORES * M, M], F32,
                            addr_space="Shared" if NCORES > 4 else "Local")

    import contextlib
    es = contextlib.ExitStack()
    sb = lambda name, shape, dt: es.enter_context(nc.sbuf_tensor(name, shape, dt))
    psum = lambda name, shape, dt=F32: es.enter_context(nc.psum_tensor(name, shape, dt))
    sem = lambda name: es.enter_context(nc.semaphore(name))

    stream = sb("stream", [128, NT, H], F32)
    if NT >= 32:
        stg, SOFF, stage_in_stream = stream, 16, True
    else:
        stg, SOFF = sb("wstage", [128, 16, H], F32), 0
        stage_in_stream = False
    qt_fm = sb("qt_fm", [128, MC, NT * TILE], BF16)
    swt = sb("swt", [128, HC, H], BF16)
    wqk = sb("wqk", [128, HC, 2 * M], BF16)
    wv = sb("wv", [128, HC, M], BF16)
    wog = sb("wog", [128, MC, H], BF16)
    weff = sb("weff", [128, MC, H], BF16)
    m0t_sb = sb("m0t_sb", [128, MC, M], F32)
    gbuf = [sb(f"gbuf{i}", [128, MC, M], F32) for i in range(2)]
    zb = [sb(f"zb{i}", [128, H], BF16) for i in range(4)]
    zbc = [sb(f"zbc{i}", [128, H], BF16) for i in range(2)]
    zf1 = [sb(f"zf1_{i}", [128, HC, 128], BF16) for i in range(2)]
    zf2 = [sb(f"zf2_{i}", [128, HC, 128], BF16) for i in range(2)]
    qtb = [sb(f"qtb{i}", [128, M], BF16) for i in range(2)]
    khb = [sb(f"khb{i}", [128, M], BF16) for i in range(2)]
    vvb = [sb(f"vvb{i}", [128, M], BF16) for i in range(2)]
    kf = [sb(f"kf{i}", [128, MC, 128], BF16) for i in range(2)]
    mskT = [sb(f"mskT{i}", [128, 128], BF16) for i in range(2)]
    ssb = [sb(f"ssb{i}", [128, MC, M], BF16) for i in range(2)]
    rb = [sb(f"rb{i}", [128, MC, 128], BF16) for i in range(2)]
    ss_f32 = sb("ss_f32", [128, MC, M], F32)
    ss_ext = sb("ss_ext", [128, MC, M], F32)
    sxt = sb("sxt", [128, MC, M], BF16)
    mask_sb = sb("mask_sb", [128, 128], F32)
    idb_sb = sb("idb_sb", [128, 128], BF16)
    smalls_sb = sb("smalls_sb", [1, 128], F32)
    gate_col = sb("gate_col", [128, 1], F32)
    qscale = sb("qscale", [128, NT], F32)
    kscale = sb("kscale", [128, NT], F32)
    wcols = sb("wcols", [128, 9], F32)
    qbase = sb("qbase", [128, 1], F32)
    kbase = sb("kbase", [128, 1], F32)
    onesrow = sb("onesrow", [1, 128], F32)
    cells = sb("cells", [1, 16], F32)
    qpat_in = sb("qpat_in", [1, 128], F32)
    qpat_row = sb("qpat_row", [1, 128], F32)
    kpat_row = sb("kpat_row", [1, 128], F32)
    d8in = sb("d8in", [1, NT], F32)
    d8row = sb("d8row", [1, NT], F32)
    dk8row = sb("dk8row", [1, NT], F32)
    p8in = sb("p8in", [1, 8], F32)
    p_row = sb("p_row", [1, 8], F32)
    qr_row = sb("qr_row", [1, 8], F32)
    t8 = sb("t8", [1, 8], F32)
    w9 = sb("w9", [1, 9], F32)
    statc = [sb(f"statc{i}", [128, 16], F32) for i in range(4)]
    s1f32 = sb("s1f32", [128, HC], F32)
    s1bf = sb("s1bf", [128, HC], BF16)

    def _col(idx):
        return [statc[k][:, idx:idx + 1] for k in range(4)]
    sumc, sumsq, mneg, ve, msq, s0c = (_col(0), _col(1), _col(2), _col(3),
                                       _col(4), _col(5))
    r0c, tc_, invh, inv2c, qcol, kcol = (_col(6), _col(7), _col(8), _col(9),
                                         _col(10), _col(11))
    sump, tmpc = _col(12), _col(13)

    psum_ss = psum("psum_ss", [128, MC, M])
    psum_qkv = psum("psum_qkv", [128, 2, 512])
    psum_big = psum("psum_big", [128, 2, 512])
    psum_att = [psum(f"psum_att{i}", [128, 3, 128]) for i in range(2)]
    psum_sum = psum("psum_sum", [128, 2])
    flat = lambda t: t.ap().rearrange("p a b -> p (a b)")

    swl = sem("swl")
    sxg = [sem(f"sxg{g}") for g in range(NG)]
    sst = sem("sst")
    scc = sem("scc")
    sccg = sem("sccg")
    scg = [sem("scg0"), sem("scg1")]
    suT, suV, suS = sem("suT"), sem("suV"), sem("suS")
    stz1 = [sem("stz1a"), sem("stz1b")]
    stz2 = [sem("stz2a"), sem("stz2b")]
    stz3 = [sem("stz3a"), sem("stz3b")]
    stqk = [sem("stqka"), sem("stqkb")]
    stsx = sem("stsx")
    tT, tV, tS = sem("tT"), sem("tV"), sem("tS")

    SWL_ALL = 8 * 16
    SUV_ALL = 15
    SUS_ALL = 3
    NP = NCORES
    PIDX = 3 + int(np.log2(NT))
    assert 2 ** (PIDX - 3) == NT, "NT must be a power of two"

    def _ord(body, lo, hi, prologue=(), epilogue=()):
        out = list(prologue)
        for j in range(lo, hi):
            for g, dj in body:
                jj = j + dj
                if 0 <= jj < NT:
                    out.append((g, jj))
        out += list(epilogue)
        return out

    T_ORDER = _ord([("Tb", 0), ("Tg", -1), ("Th", -1),
                    ("Ti", -1), ("Tj", -1), ("Td", 0), ("Te", 0)],
                   -1, NT + 1)
    T_ORDER += [("TB2", 0), ("TB3", 0)]
    T_ORDER += _ord([("Ca", 0), ("Ch", -2)], 0, NT + 2)

    V_ORDER = _ord([("Vc", 0), ("Vd1", 0), ("Vg", -1), ("Vh", -1),
                    ("Vw", 0), ("Vi", -1), ("Vj", -1), ("Vd2", 0),
                    ("Va1", 2), ("Va2", 2)],
                   -2, NT + 1)
    for cp in range(NP):
        V_ORDER.append(("VBc", cp))
    V_ORDER += [("VB3a", 0), ("VB3b", 0)]
    V_ORDER += _ord([("Cb", 0), ("Cc1", 0), ("Cw", 0), ("Cc2", 0),
                     ("Ci", -2)],
                    0, NT + 2)

    S_ORDER = _ord([("Sf", -1), ("Sg", -1), ("Sh", -1), ("Se", -1),
                    ("Sc", 0), ("Sd", 0), ("Sq2", 0), ("Sb", 2), ("Sq1", 2),
                    ("Sa", 2)],
                   -2, NT + 1)
    S_ORDER += _ord([("Cd", 0), ("Ce", 0), ("Cq", 0)], 0, NT)
    assert len(T_ORDER) == len(set(T_ORDER))
    assert len(V_ORDER) == len(set(V_ORDER))
    assert len(S_ORDER) == len(set(S_ORDER))

    mkT = {g: i + 1 for i, g in enumerate(T_ORDER)}
    mkV = {g: i + 1 for i, g in enumerate(V_ORDER)}
    mkS = {g: i + 1 for i, g in enumerate(S_ORDER)}

    def wT(e, g, j=0):
        if (g, j) in mkT:
            e.wait_ge(tT, mkT[(g, j)])

    def wV(e, g, j=0):
        if (g, j) in mkV:
            e.wait_ge(tV, mkV[(g, j)])

    def wS(e, g, j=0):
        if (g, j) in mkS:
            e.wait_ge(tS, mkS[(g, j)])

    xv = x_d.rearrange("(j p) h -> p j h", p=128)
    yv = y_d.rearrange("(j p) h -> p j h", p=128)

    with nc.Block() as block:

        @block.sync
        def _(e):
            e.dma_start(out=stream[:, 0:4, :],
                        in_=xv[:, 0:4, :]).then_inc(sxg[0], 16)
            e.dma_start(out=mask_sb[:, :], in_=mask_d[:, :]).then_inc(swl, 16)
            e.dma_start(out=idb_sb[:, :], in_=idb_d[:, :]).then_inc(swl, 16)
            e.dma_start(out=smalls_sb[:, :], in_=smalls_d[:, :]).then_inc(swl, 16)
            e.dma_start(out=m0t_sb[:, :, :],
                        in_=m0t_d.rearrange("(c p) i -> p c i", p=128)).then_inc(swl, 16)
            e.dma_start(out=stg[:, SOFF:SOFF + 8, :],
                        in_=wt_d.rearrange("(c p) i -> p c i", p=128)).then_inc(swl, 16)
            e.dma_start(out=stg[:, SOFF + 8:SOFF + 12, :].rearrange("p a (u i) -> p (a u) i", u=2),
                        in_=wqkt_d.rearrange("(c p) i -> p c i", p=128)).then_inc(swl, 16)
            e.dma_start(out=stg[:, SOFF + 12:SOFF + 14, :].rearrange("p a (u i) -> p (a u) i", u=4),
                        in_=wvt_d.rearrange("(c p) i -> p c i", p=128)).then_inc(swl, 16)
            e.dma_start(out=stg[:, SOFF + 14:SOFF + 16, :],
                        in_=wot_d.rearrange("(c p) i -> p c i", p=128)).then_inc(swl, 16)
            for g in range(1, NG):
                if stage_in_stream and g == 4:
                    e.wait_ge(suS, SUS_ALL)
                    e.wait_ge(suV, 9)
                e.dma_start(out=stream[:, 4 * g:4 * g + 4, :],
                            in_=xv[:, 4 * g:4 * g + 4, :]).then_inc(sxg[g], 16)
            for j in (0, 1):
                wS(e, "Sa", j)
                e.dma_start_transpose(zf1[j & 1][:, :, :],
                                      zb[j & 3][:, :]).then_inc(stz1[j & 1], 16)
            for j in range(NT):
                r2 = j & 1
                wS(e, "Sc", j)
                wT(e, "Td", j - 2)
                e.dma_start_transpose(zf2[r2][:, :, :],
                                      zb[j & 3][:, :]).then_inc(stz2[r2], 16)
                if j + 2 < NT:
                    wS(e, "Sa", j + 2)
                    wT(e, "Tb", j)
                    e.dma_start_transpose(zf1[j & 1][:, :, :],
                                          zb[(j + 2) & 3][:, :]).then_inc(stz1[j & 1], 16)
                wS(e, "Sf", j)
                wT(e, "Tg", j - 2)
                e.dma_start_transpose(
                    qt_fm[:, :, j * TILE:(j + 1) * TILE],
                    qtb[r2][:, :]).then_inc(stqk[r2], 16)
                wS(e, "Sg", j)
                e.dma_start_transpose(kf[r2][:, :, :],
                                      khb[r2][:, :]).then_inc(stqk[r2], 16)
            wV(e, "Vj", NT - 1)
            e.dma_start(out=cc_in.ap().rearrange("(c p) i -> p c i", p=128),
                        in_=ss_f32[:, :, :]).then_inc(scc, 16)
            e.wait_ge(sccg, 1)
            gview = cc_out.ap().rearrange("(g p) i -> p g i", p=128)
            for cp in range(NP):
                if cp >= 2:
                    wV(e, "VBc", cp - 2)
                e.dma_start(out=gbuf[cp % 2][:, :, :],
                            in_=gview[:, 2 * cp:2 * cp + 2, :]).then_inc(scg[cp % 2], 16)
            wV(e, "VBc", NP - 1)
            for jc in range(MC):
                e.dma_start_transpose(
                    sxt[:, :, jc * 128:(jc + 1) * 128],
                    ssb[0][:, jc, :]).then_inc(stsx, 16)
            for j in range(NT):
                r2 = j & 1
                wS(e, "Ce", j)
                if j < 2:
                    wT(e, "Tb", NT - 2 + j)
                else:
                    wT(e, "Ch", j - 2)
                e.dma_start_transpose(zf1[r2][:, :, :],
                                      zbc[r2][:, :]).then_inc(stz3[r2], 16)
            for g in range(NG):
                wV(e, "Ci", 4 * g + 3)
                e.dma_start(out=yv[:, 4 * g:4 * g + 4, :],
                            in_=stream[:, 4 * g:4 * g + 4, :]).then_inc(sst, 16)
            e.wait_ge(sst, 16 * NG)

        @block.tensor
        def _(e):
            e.wait_ge(suV, 4)
            e.matmul(psum_att[0][:, 0, 0:1], qpat_row[0:1, :], cells[0:1, 14:15],
                     start=True, stop=True)
            e.matmul(psum_att[0][:, 0, 1:2], kpat_row[0:1, :], cells[0:1, 14:15],
                     start=True, stop=True).then_inc(suT, 1)
            e.wait_ge(suV, 5)
            e.wait_ge(suV, 10)
            e.matmul(psum_att[0][:, 2, 0:NT], onesrow[0:1, :], d8row[0:1, :],
                     start=True, stop=True)
            e.matmul(psum_att[0][:, 2, NT:2 * NT], onesrow[0:1, :], dk8row[0:1, :],
                     start=True, stop=True).then_inc(suT, 1)
            e.wait_ge(suV, 7)
            e.wait_ge(suV, 12)
            e.matmul(psum_att[0][:, 1, 0:9], onesrow[0:1, :], w9[0:1, :],
                     start=True, stop=True).then_inc(suT, 1)
            e.wait_ge(suS, 1)
            e.wait_ge(suV, 13)
            e.matmul(psum_att[0][:, 0, 2:3], onesrow[0:1, :], cells[0:1, 13:14],
                     start=True, stop=True).then_inc(suT, 1)
            e.wait_ge(swl, SWL_ALL)
            e.wait_ge(suS, SUS_ALL)
            e.wait_ge(suV, SUV_ALL)

            def Tb(j):
                r = j & 1
                c = j & 1
                e.wait_ge(stz1[r], 16 * (j // 2 + 1))
                wV(e, "Vc", j - 1)
                wV(e, "Vd1", j - 2)
                for kc in range(HC):
                    for oc in range(2):
                        e.matmul(psum_big[:, oc, :], zf1[r][:, kc, :],
                                 swt[:, kc, oc * 512:(oc + 1) * 512],
                                 start=(kc == 0), stop=(kc == HC - 1))
                    mm = e.matmul(psum_sum[:, c:c + 1], zf1[r][:, kc, :],
                                  s1bf[:, kc:kc + 1],
                                  start=(kc == 0), stop=(kc == HC - 1))
                mm.then_inc(tT, 1)

            def Td(j):
                r = j & 1
                e.wait_ge(stz2[r], 16 * (j // 2 + 1))
                wV(e, "Vi", j - 1)
                for kc in range(HC):
                    mm = e.matmul(psum_qkv[:, 0, :], zf2[r][:, kc, :], wqk[:, kc, :],
                                  start=(kc == 0), stop=(kc == HC - 1))
                mm.then_inc(tT, 1)

            def Te(j):
                r = j & 1
                for kc in range(HC):
                    mm = e.matmul(psum_qkv[:, 1, 0:M], zf2[r][:, kc, :], wv[:, kc, :],
                                  start=(kc == 0), stop=(kc == HC - 1))
                mm.then_inc(tT, 1)

            def Tg(j):
                r = j & 1
                c0 = j * TILE
                e.wait_ge(stqk[r], 32 * (j // 2 + 1))
                wV(e, "Vh", j - 2)
                for c in range(MC):
                    mm = e.matmul(psum_att[r][:, 0, :], kf[r][:, c, :],
                                  qt_fm[:, c, c0:c0 + 128],
                                  start=(c == 0), stop=(c == MC - 1))
                mm.then_inc(tT, 1)

            def Th(j):
                r = j & 1
                c0 = j * TILE
                wV(e, "Vg", j)
                wS(e, "Se", j)
                wV(e, "Vh", j - 2)
                for mi in range(MC):
                    for mj in range(MC):
                        e.matmul(psum_att[r][:, 1 + mi, :],
                                 ssb[r][:, mj, mi * 128:(mi + 1) * 128],
                                 qt_fm[:, mj, c0:c0 + 128],
                                 start=(mi == 0 and mj == 0), stop=False)
                    mm = e.matmul(psum_att[r][:, 1 + mi, :], vvb[r][:, mi * 128:(mi + 1) * 128],
                                  mskT[r][:, :], start=False, stop=(mi == MC - 1))
                mm.then_inc(tT, 1)

            def Ti(j):
                r = j & 1
                wS(e, "Sh", j)
                wV(e, "Vj", j - 1)
                for mk in range(MC):
                    mm = e.matmul(psum_ss[:, mk, :], khb[r][:, mk * 128:(mk + 1) * 128],
                                  vvb[r][:, :], start=(mk == 0), stop=(mk == MC - 1))
                mm.then_inc(tT, 1)

            def Tj(j):
                r = j & 1
                wV(e, "Vh", j)
                wS(e, "Sh", j)
                for kc in range(MC):
                    for oc in range(2):
                        mm = e.matmul(psum_qkv[:, oc, :], rb[r][:, kc, :],
                                      wog[:, kc, oc * 512:(oc + 1) * 512],
                                      start=(kc == 0), stop=(kc == MC - 1))
                mm.then_inc(tT, 1)

            for g, j in T_ORDER:
                if g == "Tb":
                    Tb(j)
                elif g == "Td":
                    Td(j)
                elif g == "Te":
                    Te(j)
                elif g == "Tg":
                    Tg(j)
                elif g == "Th":
                    Th(j)
                elif g == "Ti":
                    Ti(j)
                elif g == "Tj":
                    Tj(j)
                elif g == "TB2":
                    e.wait_ge(stsx, 16 * MC)
                    wV(e, "Vc", NT - 1)
                    for ic in range(MC):
                        for hc2 in range(2):
                            mm = e.matmul(psum_big[:, hc2, :], sxt[:, ic, 0:128],
                                          wog[:, ic, hc2 * 512:(hc2 + 1) * 512],
                                          start=(ic == 0), stop=(ic == MC - 1))
                    mm.then_inc(tT, 1)
                elif g == "TB3":
                    wV(e, "VB3a", 0)
                    for ic in range(MC):
                        for hc2 in range(2):
                            mm = e.matmul(psum_big[:, hc2, :], sxt[:, ic, 128:256],
                                          wog[:, ic, hc2 * 512:(hc2 + 1) * 512],
                                          start=(ic == 0), stop=(ic == MC - 1))
                    mm.then_inc(tT, 1)
                elif g == "Ca":
                    c0 = j * TILE
                    if j == 0:
                        wV(e, "VB3b", 0)
                        wV(e, "Vi", NT - 1)
                    else:
                        wV(e, "Cb", j - 1)
                    for mjc in range(MC):
                        for hc2 in range(2):
                            mm = e.matmul(psum_qkv[:, hc2, :], qt_fm[:, mjc, c0:c0 + 128],
                                          weff[:, mjc, hc2 * 512:(hc2 + 1) * 512],
                                          start=(mjc == 0), stop=(mjc == MC - 1))
                    mm.then_inc(tT, 1)
                elif g == "Ch":
                    r = j & 1
                    e.wait_ge(stz3[r], 16 * (j // 2 + 1))
                    wV(e, "Ci", j - 1)
                    if j == 0:
                        wV(e, "VB3b", 0)
                    for kc in range(HC):
                        for oc in range(2):
                            mm = e.matmul(psum_big[:, oc, :], zf1[r][:, kc, :],
                                          swt[:, kc, oc * 512:(oc + 1) * 512],
                                          start=(kc == 0), stop=(kc == HC - 1))
                    mm.then_inc(tT, 1)

        @block.vector
        def _(e):
            AXX = AX.X
            e.memset(onesrow[0:1, :], 1.0)
            e.memset(cells[0:1, 14:15], 1.0)
            e.drain().then_inc(suV, 1)
            e.tensor_scalar(cells[0:1, 0:1], cells[0:1, 14:15], DECAY, None, ALU.mult)
            e.drain()
            for i in range(8):
                e.tensor_mul(cells[0:1, i + 1:i + 2], cells[0:1, i:i + 1],
                             cells[0:1, i:i + 1])
                e.drain()
            e.reciprocal(cells[0:1, 9:10], cells[0:1, 0:1])
            e.reciprocal(cells[0:1, 10:11], cells[0:1, 3:4])
            e.reciprocal(cells[0:1, 11:12], cells[0:1, PIDX:PIDX + 1])
            e.drain().then_inc(suV, 1)
            e.memset(qpat_in[0:1, :], 1.0)
            e.drain()
            e.tensor_scalar(qpat_in[0:1, 0:128:16], qpat_in[0:1, 0:128:16],
                            DECAY, None, ALU.mult)
            e.drain()
            e.tensor_tensor_scan(qpat_row[0:1, :], qpat_in[0:1, :], qpat_in[0:1, :],
                                 cells[0:1, 9:10], ALU.mult, ALU.bypass)
            e.drain().then_inc(suV, 1)
            e.reciprocal(kpat_row[0:1, :], qpat_row[0:1, :])
            e.drain()
            e.tensor_scalar(kpat_row[0:1, :], kpat_row[0:1, :],
                            cells[0:1, 9:10], None, ALU.mult)
            e.drain().then_inc(suV, 1)
            e.memset(d8in[0:1, :], 1.0)
            e.drain()
            e.tensor_scalar(d8in[0:1, :], d8in[0:1, :], cells[0:1, 3:4], None, ALU.mult)
            e.drain()
            e.tensor_tensor_scan(d8row[0:1, :], d8in[0:1, :], d8in[0:1, :],
                                 cells[0:1, 10:11], ALU.mult, ALU.bypass)
            e.drain()
            e.reciprocal(dk8row[0:1, :], d8row[0:1, :])
            e.drain().then_inc(suV, 1)
            e.memset(p8in[0:1, :], 1.0)
            e.drain()
            e.tensor_scalar(p8in[0:1, :], p8in[0:1, :], cells[0:1, PIDX:PIDX + 1],
                            None, ALU.mult)
            e.drain()
            e.tensor_tensor_scan(p_row[0:1, :], p8in[0:1, :], p8in[0:1, :],
                                 cells[0:1, 11:12], ALU.mult, ALU.bypass)
            e.drain()
            e.reciprocal(qr_row[0:1, :], p_row[0:1, :])
            e.drain().then_inc(suV, 1)
            e.wait_ge(swl, SWL_ALL)
            e.tensor_mul(t8[0:1, :], smalls_sb[0:1, 0:8], p_row[0:1, :])
            e.drain()
            e.reduce_sum(cells[0:1, 15:16], t8[0:1, :], axis=AXX)
            e.drain()
            e.tensor_mul(w9[0:1, 0:8], qr_row[0:1, :], smalls_sb[0:1, 8:16])
            e.drain()
            e.tensor_scalar(w9[0:1, 0:8], w9[0:1, 0:8], cells[0:1, 15:16], 1.0 / 16.0,
                            ALU.mult, ALU.mult)
            e.tensor_copy(w9[0:1, 8:9], cells[0:1, 15:16])
            e.drain().then_inc(suV, 1)
            e.tensor_copy(flat(wqk),
                          stg[:, SOFF + 8:SOFF + 12, :].rearrange("p a b -> p (a b)"))
            e.drain().then_inc(suV, 1)
            e.tensor_copy(flat(wv),
                          stg[:, SOFF + 12:SOFF + 14, :].rearrange("p a b -> p (a b)"))
            e.drain().then_inc(suV, 1)
            e.wait_ge(suT, 1)
            e.tensor_copy(qbase[:, :], psum_att[0][:, 0, 0:1])
            e.tensor_copy(kbase[:, :], psum_att[0][:, 0, 1:2])
            e.drain().then_inc(suV, 1)
            e.wait_ge(suT, 2)
            e.tensor_scalar(qscale[:, :], psum_att[0][:, 2, 0:NT], qbase[:, :],
                            None, ALU.mult)
            e.drain().then_inc(suV, 1)
            e.tensor_scalar(kscale[:, :], psum_att[0][:, 2, NT:2 * NT], kbase[:, :],
                            None, ALU.mult)
            e.drain().then_inc(suV, 1)
            e.wait_ge(suT, 3)
            e.tensor_copy(wcols[:, :], psum_att[0][:, 1, 0:9])
            e.drain().then_inc(suV, 1)
            e.wait_ge(suT, 4)
            e.tensor_copy(gate_col[:, :], psum_att[0][:, 0, 2:3])
            e.drain().then_inc(suV, 1)
            e.wait_ge(suS, 2)
            for kc in range(HC):
                e.reduce_sum(s1f32[:, kc:kc + 1], swt[:, kc, :], axis=AXX)
            e.drain()
            e.tensor_copy(s1bf[:, :], s1f32[:, :])
            e.drain().then_inc(suV, 1)

            def Va1(j):
                r = j & 3
                e.wait_ge(sxg[j // 4], 16)
                wS(e, "Sc", j - 4)
                e.reduce_sum(sumc[r], stream[:, j, :], axis=AXX)
                wS(e, "Sb", j)
                e.drain()
                e.tensor_scalar(mneg[r], sumc[r], -1.0 / H, None, ALU.mult)
                e.tensor_scalar(ve[r], sumsq[r], 1.0 / H, EPS, ALU.mult, ALU.add)
                e.drain()
                e.tensor_mul(msq[r], mneg[r], mneg[r])
                e.drain()
                e.tensor_sub(ve[r], ve[r], msq[r]).then_inc(tV, 1)

            def Va2(j):
                r = j & 3
                wS(e, "Sq1", j)
                e.reciprocal(inv2c[r], s0c[r])
                e.drain()
                e.tensor_scalar(invh[r], inv2c[r], 0.5, None, ALU.mult).then_inc(tV, 1)

            def Vc(j):
                r = j & 3
                wT(e, "Tb", j)
                e.scalar_tensor_tensor(stream[:, j, :], flat(psum_big),
                                       invh[r], stream[:, j, :],
                                       ALU.mult, ALU.add).then_inc(tV, 1)

            def Vd1(j):
                r = j & 3
                e.tensor_scalar(tmpc[r], psum_sum[:, (j & 1):(j & 1) + 1],
                                invh[r], -1.0 / H, ALU.mult, ALU.mult)
                e.drain()
                e.scalar_tensor_tensor(mneg[r], sumc[r], -1.0 / H, tmpc[r],
                                       ALU.mult, ALU.add).then_inc(tV, 1)

            def Vw(j):
                r = j & 3
                wS(e, "Sd", j)
                e.tensor_scalar(ve[r], sumsq[r], 1.0 / H, EPS, ALU.mult, ALU.add)
                e.drain()
                e.tensor_mul(msq[r], mneg[r], mneg[r])
                e.drain()
                e.tensor_sub(ve[r], ve[r], msq[r]).then_inc(tV, 1)

            def Vd2(j):
                r = j & 3
                wS(e, "Sq2", j)
                e.reciprocal(inv2c[r], s0c[r])
                e.drain()
                e.tensor_scalar(qcol[r], qscale[:, j:j + 1], inv2c[r], None, ALU.mult)
                e.tensor_scalar(kcol[r], kscale[:, j:j + 1], inv2c[r],
                                None, ALU.mult).then_inc(tV, 1)

            def Vg(j):
                r = j & 1
                wT(e, "Tg", j)
                e.tensor_mul(mskT[r][:, :], psum_att[r][:, 0, :],
                             mask_sb[:, :]).then_inc(tV, 1)

            def Vh(j):
                r = j & 1
                wT(e, "Th", j)
                e.tensor_copy(flat(rb[r]),
                              psum_att[r][:, 1:3, :].rearrange("p a b -> p (a b)")
                              ).then_inc(tV, 1)

            def Vi(j):
                wT(e, "Tj", j)
                e.tensor_add(stream[:, j, :], stream[:, j, :],
                             flat(psum_qkv)).then_inc(tV, 1)

            def Vj(j):
                wT(e, "Ti", j)
                if j == 0:
                    e.tensor_copy(flat(ss_f32), flat(psum_ss)).then_inc(tV, 1)
                else:
                    e.tensor_add(flat(ss_f32), flat(ss_f32),
                                 flat(psum_ss)).then_inc(tV, 1)

            for g, j in V_ORDER:
                if g == "Va1":
                    Va1(j)
                elif g == "Va2":
                    Va2(j)
                elif g == "Vc":
                    Vc(j)
                elif g == "Vd1":
                    Vd1(j)
                elif g == "Vw":
                    Vw(j)
                elif g == "Vd2":
                    Vd2(j)
                elif g == "Vg":
                    Vg(j)
                elif g == "Vh":
                    Vh(j)
                elif g == "Vi":
                    Vi(j)
                elif g == "Vj":
                    Vj(j)
                elif g == "VBc":
                    cp = j
                    e.wait_ge(scg[cp % 2], 16 * (cp // 2 + 1))
                    if cp == 0:
                        e.tensor_scalar(flat(ss_ext), flat(m0t_sb),
                                        wcols[:, 8:9], None, ALU.mult)
                        e.drain()
                    e.scalar_tensor_tensor(flat(ss_ext), flat(gbuf[cp % 2]),
                                           wcols[:, cp:cp + 1], flat(ss_ext),
                                           ALU.mult, ALU.add)
                    e.drain()
                    if cp == NP - 1:
                        wT(e, "Th", NT - 1)
                        e.tensor_copy(flat(ssb[0]), flat(ss_ext)).then_inc(tV, 1)
                    else:
                        e.nop()
                        e.drain().then_inc(tV, 1)
                elif g == "VB3a":
                    wT(e, "TB2", 0)
                    e.tensor_copy(weff[:, 0, :], flat(psum_big)).then_inc(tV, 1)
                elif g == "VB3b":
                    wT(e, "TB3", 0)
                    e.tensor_copy(weff[:, 1, :], flat(psum_big)).then_inc(tV, 1)
                elif g == "Cb":
                    wT(e, "Ca", j)
                    e.drain()
                    e.tensor_add(stream[:, j, :], stream[:, j, :],
                                 flat(psum_qkv)).then_inc(tV, 1)
                elif g == "Cc1":
                    r = j & 3
                    wS(e, "Ce", j - 2)
                    e.drain()
                    e.reduce_sum(sumc[r], stream[:, j, :], axis=AXX)
                    e.drain()
                    e.tensor_scalar(mneg[r], sumc[r], -1.0 / H,
                                    None, ALU.mult).then_inc(tV, 1)
                elif g == "Cw":
                    r = j & 3
                    wS(e, "Cd", j)
                    e.tensor_scalar(ve[r], sumsq[r], 1.0 / H, EPS, ALU.mult, ALU.add)
                    e.drain()
                    e.tensor_mul(msq[r], mneg[r], mneg[r])
                    e.drain()
                    e.tensor_sub(ve[r], ve[r], msq[r]).then_inc(tV, 1)
                elif g == "Cc2":
                    r = j & 3
                    wS(e, "Cq", j)
                    e.reciprocal(inv2c[r], s0c[r])
                    e.drain()
                    e.tensor_scalar(invh[r], inv2c[r], 0.5,
                                    None, ALU.mult).then_inc(tV, 1)
                elif g == "Ci":
                    r = j & 3
                    wT(e, "Ch", j)
                    e.scalar_tensor_tensor(stream[:, j, :], flat(psum_big),
                                           invh[r], stream[:, j, :],
                                           ALU.mult, ALU.add).then_inc(tV, 1)

        @block.scalar
        def _(e):
            e.wait_ge(swl, SWL_ALL)
            e.activation(cells[0:1, 13:14], smalls_sb[0:1, 16:17],
                         ACT.Sigmoid).then_inc(suS, 1)
            e.activation(flat(swt), stg[:, SOFF:SOFF + 8, :].rearrange("p a b -> p (a b)"),
                         ACT.Sign).then_inc(suS, 1)
            e.wait_ge(suV, SUV_ALL)
            for c in range(MC):
                aa = e.activation(wog[:, c, :], stg[:, SOFF + 14 + c, :], ACT.Copy,
                                  scale=gate_col[:, :])
            aa.then_inc(suS, 1)

            def Sb(j):
                r = j & 3
                e.wait_ge(sxg[j // 4], 16)
                if j >= 4:
                    e.wait_ge(stz2[j & 1], 16 * ((j - 4) // 2 + 1))
                wV(e, "Vd1", j - 4)
                e.activation(zb[r][:, :], stream[:, j, :], ACT.Square,
                             accum_out=sumsq[r]).then_inc(tS, 1)

            def Sq1(j):
                r = j & 3
                wV(e, "Va1", j)
                wV(e, "Va2", j - 4)
                e.activation(s0c[r], ve[r], ACT.Sqrt).then_inc(tS, 1)

            def Sa(j):
                r = j & 3
                wV(e, "Va1", j)
                e.activation(zb[r][:, :], stream[:, j, :], ACT.Identity,
                             bias=mneg[r], scale=1.0).then_inc(tS, 1)

            def Sd(j):
                r = j & 3
                wV(e, "Vc", j)
                e.activation(zbc[j & 1][:, :], stream[:, j, :], ACT.Square,
                             accum_out=sumsq[r]).then_inc(tS, 1)

            def Sq2(j):
                r = j & 3
                wV(e, "Vw", j)
                wV(e, "Va2", j)
                e.activation(s0c[r], ve[r], ACT.Sqrt).then_inc(tS, 1)

            def Sc(j):
                r = j & 3
                wV(e, "Vd1", j)
                e.wait_ge(stz1[j & 1], 16 * (j // 2 + 1))
                e.activation(zb[r][:, :], stream[:, j, :], ACT.Identity,
                             bias=mneg[r], scale=1.0).then_inc(tS, 1)

            def Se(j):
                r = j & 1
                if j == 0:
                    e.memzero(flat(ssb[0])).then_inc(tS, 1)
                else:
                    wV(e, "Vj", j - 1)
                    wT(e, "Th", j - 2)
                    e.activation(flat(ssb[r]), flat(ss_f32), ACT.Copy,
                                 scale=1.0 / 16.0).then_inc(tS, 1)

            def Sf(j):
                r = j & 1
                wT(e, "Td", j)
                wV(e, "Vd2", j)
                if j >= 2:
                    e.wait_ge(stqk[r], 32 * ((j - 2) // 2 + 1))
                e.activation(qtb[r][:, :], psum_qkv[:, 0, 0:M], ACT.Copy,
                             scale=qcol[j & 3]).then_inc(tS, 1)

            def Sg(j):
                r = j & 1
                e.activation(khb[r][:, :], psum_qkv[:, 0, M:2 * M], ACT.Copy,
                             scale=kcol[j & 3]).then_inc(tS, 1)

            def Sh(j):
                r = j & 1
                wT(e, "Te", j)
                wT(e, "Ti", j - 2)
                e.activation(vvb[r][:, :], psum_qkv[:, 1, 0:M], ACT.Copy,
                             scale=inv2c[j & 3]).then_inc(tS, 1)

            for g, j in S_ORDER:
                if g == "Sb":
                    Sb(j)
                elif g == "Sq1":
                    Sq1(j)
                elif g == "Sa":
                    Sa(j)
                elif g == "Sd":
                    Sd(j)
                elif g == "Sq2":
                    Sq2(j)
                elif g == "Sc":
                    Sc(j)
                elif g == "Se":
                    Se(j)
                elif g == "Sf":
                    Sf(j)
                elif g == "Sg":
                    Sg(j)
                elif g == "Sh":
                    Sh(j)
                elif g == "Cd":
                    r = j & 1
                    r4 = j & 3
                    wV(e, "Cb", j)
                    if j >= 2:
                        e.wait_ge(stz3[r], 16 * ((j - 2) // 2 + 1))
                    wV(e, "Cc1", j - 2)
                    e.activation(zbc[r][:, :], stream[:, j, :], ACT.Square,
                                 accum_out=sumsq[r4]).then_inc(tS, 1)
                elif g == "Cq":
                    r = j & 3
                    wV(e, "Cw", j)
                    wV(e, "Cc2", j - 2)
                    e.activation(s0c[r], ve[r], ACT.Sqrt).then_inc(tS, 1)
                elif g == "Ce":
                    r = j & 1
                    wV(e, "Cc1", j)
                    e.drain()
                    e.activation(zbc[r][:, :], stream[:, j, :], ACT.Identity,
                                 bias=mneg[j & 3], scale=1.0).then_inc(tS, 1)

        @block.gpsimd
        def _(e):
            e.wait_ge(scc, 16)
            e.collective_compute(
                "AllGather", ALU.bypass,
                replica_groups=[list(range(NCORES))],
                ins=[cc_in.ap().opt()],
                outs=[cc_out.ap().opt()],
            ).then_inc(sccg, 1)

    es.close()
    return nc


def _host_layout(x, Wq, Wk, Wv, Wo, gate, M0, W, NT=32, NCORES=8):
    TPC = NT * TILE // B
    xs = []
    for c in range(NCORES):
        xc = np.ascontiguousarray(
            x[:, c * TPC:(c + 1) * TPC, :].transpose(1, 0, 2)).reshape(NT * TILE, H)
        xs.append(xc)
    wt = np.ascontiguousarray(W.T)
    wqkt = np.ascontiguousarray(np.concatenate([Wq.T, Wk.T], axis=1))
    wvt = np.ascontiguousarray(Wv.T)
    wot = np.ascontiguousarray(Wo.T)
    m0t = np.ascontiguousarray(M0.T)
    wtok = np.arange(TILE) // B
    mask = ((wtok[:, None] < wtok[None, :]).astype(np.float32) / np.float32(16.0))
    idb = np.eye(128).astype(ml_dtypes.bfloat16)
    gate_v = float(np.asarray(gate, np.float32).reshape(-1)[0])
    maps = []
    for c in range(NCORES):
        smalls = np.zeros((1, 128), np.float32)
        smalls[0, c] = 1.0
        smalls[0, 8:8 + c] = 1.0
        smalls[0, 16] = gate_v
        maps.append({
            "x": xs[c], "wt": wt, "wqkt": wqkt, "wvt": wvt, "wot": wot,
            "m0t": m0t, "mask": mask, "idb": idb, "smalls": smalls,
        })
    return maps


def kernel(L0_attn_x, L0_attn_Wq, L0_attn_Wk, L0_attn_Wv, L0_attn_Wo,
           L0_attn_gate, L0_attn_M0, L0_ffn_weight):
    NT, NCORES = 32, 8
    x = np.asarray(L0_attn_x, np.float32)
    maps = _host_layout(x, np.asarray(L0_attn_Wq, np.float32),
                        np.asarray(L0_attn_Wk, np.float32),
                        np.asarray(L0_attn_Wv, np.float32),
                        np.asarray(L0_attn_Wo, np.float32),
                        np.asarray(L0_attn_gate, np.float32),
                        np.asarray(L0_attn_M0, np.float32),
                        np.asarray(L0_ffn_weight, np.float32), NT, NCORES)
    nc = build_nc(NT, NCORES)
    res = run_bass_kernel_spmd(nc, maps, core_ids=list(range(NCORES)))
    Bv, T = x.shape[0], x.shape[1]
    TPC = T // NCORES
    out = np.empty((Bv, T, H), np.float32)
    for c in range(NCORES):
        yc = np.asarray(res.results[c]["y"]).reshape(TPC, B, H)
        out[:, c * TPC:(c + 1) * TPC, :] = yc.transpose(1, 0, 2)
    return out


# version 20
# speedup vs baseline: 1.0064x; 1.0064x over previous
import numpy as np
import ml_dtypes
import concourse.bass as bass
import concourse.mybir as mybir
from concourse.bass_utils import run_bass_kernel_spmd

F32 = mybir.dt.float32
BF16 = mybir.dt.bfloat16
AX = mybir.AxisListType
ALU = mybir.AluOpType
ACT = mybir.ActivationFunctionType

H = 1024
M = 256
B = 16
TILE = 128
HC = H // 128
MC = M // 128
DECAY = 0.99
EPS = 1e-5


def build_nc(NT=32, NCORES=8):
    NG = NT // 4
    nc = bass.Bass("TRN2", target_bir_lowering=False, debug=False,
                   dynamic_dma_scratch_size=512)

    x_d = nc.dram_tensor("x", [NT * TILE, H], F32, kind="ExternalInput").ap()
    wt_d = nc.dram_tensor("wt", [H, H], F32, kind="ExternalInput").ap()
    wqkt_d = nc.dram_tensor("wqkt", [H, 2 * M], F32, kind="ExternalInput").ap()
    wvt_d = nc.dram_tensor("wvt", [H, M], F32, kind="ExternalInput").ap()
    wot_d = nc.dram_tensor("wot", [M, H], F32, kind="ExternalInput").ap()
    m0t_d = nc.dram_tensor("m0t", [M, M], F32, kind="ExternalInput").ap()
    mask_d = nc.dram_tensor("mask", [128, 128], F32, kind="ExternalInput").ap()
    idb_d = nc.dram_tensor("idb", [128, 128], BF16, kind="ExternalInput").ap()
    smalls_d = nc.dram_tensor("smalls", [1, 128], F32, kind="ExternalInput").ap()
    y_d = nc.dram_tensor("y", [NT * TILE, H], F32, kind="ExternalOutput").ap()
    cc_in = nc.dram_tensor("cc_in", [M, M], F32)
    cc_out = nc.dram_tensor("cc_out", [NCORES * M, M], F32,
                            addr_space="Shared" if NCORES > 4 else "Local")

    import contextlib
    es = contextlib.ExitStack()
    sb = lambda name, shape, dt: es.enter_context(nc.sbuf_tensor(name, shape, dt))
    psum = lambda name, shape, dt=F32: es.enter_context(nc.psum_tensor(name, shape, dt))
    sem = lambda name: es.enter_context(nc.semaphore(name))

    stream = sb("stream", [128, NT, H], F32)
    if NT >= 32:
        stg, SOFF, stage_in_stream = stream, 16, True
    else:
        stg, SOFF = sb("wstage", [128, 16, H], F32), 0
        stage_in_stream = False
    qt_fm = sb("qt_fm", [128, MC, NT * TILE], BF16)
    swt = sb("swt", [128, HC, H], BF16)
    wqk = sb("wqk", [128, HC, 2 * M], BF16)
    wv = sb("wv", [128, HC, M], BF16)
    wog = sb("wog", [128, MC, H], BF16)
    weff = sb("weff", [128, MC, H], BF16)
    m0t_sb = sb("m0t_sb", [128, MC, M], F32)
    gbuf = [sb(f"gbuf{i}", [128, MC, M], F32) for i in range(2)]
    zb = [sb(f"zb{i}", [128, H], BF16) for i in range(4)]
    zbc = [sb(f"zbc{i}", [128, H], BF16) for i in range(2)]
    zf1 = [sb(f"zf1_{i}", [128, HC, 128], BF16) for i in range(2)]
    zf2 = [sb(f"zf2_{i}", [128, HC, 128], BF16) for i in range(2)]
    qtb = [sb(f"qtb{i}", [128, M], BF16) for i in range(2)]
    khb = [sb(f"khb{i}", [128, M], BF16) for i in range(2)]
    vvb = [sb(f"vvb{i}", [128, M], BF16) for i in range(2)]
    kf = [sb(f"kf{i}", [128, MC, 128], BF16) for i in range(2)]
    mskT = [sb(f"mskT{i}", [128, 128], BF16) for i in range(2)]
    ssb = [sb(f"ssb{i}", [128, MC, M], BF16) for i in range(2)]
    rb = [sb(f"rb{i}", [128, MC, 128], BF16) for i in range(2)]
    ss_f32 = sb("ss_f32", [128, MC, M], F32)
    ss_ext = sb("ss_ext", [128, MC, M], F32)
    sxt = sb("sxt", [128, MC, M], BF16)
    mask_sb = sb("mask_sb", [128, 128], F32)
    idb_sb = sb("idb_sb", [128, 128], BF16)
    smalls_sb = sb("smalls_sb", [1, 128], F32)
    gate_col = sb("gate_col", [128, 1], F32)
    qscale = sb("qscale", [128, NT], F32)
    kscale = sb("kscale", [128, NT], F32)
    wcols = sb("wcols", [128, 9], F32)
    qbase = sb("qbase", [128, 1], F32)
    kbase = sb("kbase", [128, 1], F32)
    onesrow = sb("onesrow", [1, 128], F32)
    cells = sb("cells", [1, 16], F32)
    qpat_in = sb("qpat_in", [1, 128], F32)
    qpat_row = sb("qpat_row", [1, 128], F32)
    kpat_row = sb("kpat_row", [1, 128], F32)
    d8in = sb("d8in", [1, NT], F32)
    d8row = sb("d8row", [1, NT], F32)
    dk8row = sb("dk8row", [1, NT], F32)
    p8in = sb("p8in", [1, 8], F32)
    p_row = sb("p_row", [1, 8], F32)
    qr_row = sb("qr_row", [1, 8], F32)
    t8 = sb("t8", [1, 8], F32)
    w9 = sb("w9", [1, 9], F32)
    statc = [sb(f"statc{i}", [128, 16], F32) for i in range(4)]
    s1f32 = sb("s1f32", [128, HC], F32)
    s1bf = sb("s1bf", [128, HC], BF16)

    def _col(idx):
        return [statc[k][:, idx:idx + 1] for k in range(4)]
    sumc, sumsq, mneg, ve, msq, s0c = (_col(0), _col(1), _col(2), _col(3),
                                       _col(4), _col(5))
    r0c, tc_, invh, inv2c, qcol, kcol = (_col(6), _col(7), _col(8), _col(9),
                                         _col(10), _col(11))
    sump, tmpc = _col(12), _col(13)

    psum_ss = psum("psum_ss", [128, MC, M])
    psum_qkv = psum("psum_qkv", [128, 2, 512])
    psum_big = psum("psum_big", [128, 2, 512])
    psum_att = [psum(f"psum_att{i}", [128, 3, 128]) for i in range(2)]
    psum_sum = psum("psum_sum", [128, 2])
    flat = lambda t: t.ap().rearrange("p a b -> p (a b)")

    swl = sem("swl")
    sxg = [sem(f"sxg{g}") for g in range(NG)]
    sst = sem("sst")
    scc = sem("scc")
    sccg = sem("sccg")
    scg = [sem("scg0"), sem("scg1")]
    suT, suV, suS = sem("suT"), sem("suV"), sem("suS")
    stz1 = [sem("stz1a"), sem("stz1b")]
    stz2 = [sem("stz2a"), sem("stz2b")]
    stz2h = [sem("stz2ha"), sem("stz2hb")]
    stz3 = [sem("stz3a"), sem("stz3b")]
    stqk = [sem("stqka"), sem("stqkb")]
    stsx = sem("stsx")
    tT, tV, tS = sem("tT"), sem("tV"), sem("tS")

    SWL_ALL = 8 * 16
    SUV_ALL = 15
    SUS_ALL = 3
    NP = NCORES
    PIDX = 3 + int(np.log2(NT))
    assert 2 ** (PIDX - 3) == NT, "NT must be a power of two"

    def _ord(body, lo, hi, prologue=(), epilogue=()):
        out = list(prologue)
        for j in range(lo, hi):
            for g, dj in body:
                jj = j + dj
                if 0 <= jj < NT:
                    out.append((g, jj))
        out += list(epilogue)
        return out

    T_ORDER = _ord([("Tb", 0), ("Te", -1), ("Tg", -1), ("Th", -1),
                    ("Ti", -1), ("Tj", -1), ("Td", 0)],
                   -1, NT + 1)
    T_ORDER += [("TB2", 0), ("TB3", 0)]
    T_ORDER += _ord([("Ca", 0), ("Ch", -2)], 0, NT + 2)

    V_ORDER = _ord([("Vd1", 0), ("Vc", 0), ("Vc2", 0), ("Vg", -1), ("Vh", -1),
                    ("Vw", 0), ("Vi", -1), ("Vj", -1), ("Vd2", 0),
                    ("Va1", 2), ("Va2", 2)],
                   -2, NT + 1)
    for cp in range(NP):
        V_ORDER.append(("VBc", cp))
    V_ORDER += [("VB3a", 0), ("VB3b", 0)]
    V_ORDER += _ord([("Cb", 0), ("Cc1", 0), ("Cw", 0), ("Cc2", 0),
                     ("Ci", -2)],
                    0, NT + 2)

    S_ORDER = _ord([("Sf", -1), ("Sg", -1), ("Sh", -1), ("Se", -1),
                    ("Sc", 0), ("Sc2", 0), ("Sd", 0), ("Sq2", 0), ("Sb", 2),
                    ("Sq1", 2), ("Sa", 2)],
                   -2, NT + 1)
    S_ORDER += _ord([("Cd", 0), ("Ce", 0), ("Cq", 0)], 0, NT)
    assert len(T_ORDER) == len(set(T_ORDER))
    assert len(V_ORDER) == len(set(V_ORDER))
    assert len(S_ORDER) == len(set(S_ORDER))

    mkT = {g: i + 1 for i, g in enumerate(T_ORDER)}
    mkV = {g: i + 1 for i, g in enumerate(V_ORDER)}
    mkS = {g: i + 1 for i, g in enumerate(S_ORDER)}

    def wT(e, g, j=0):
        if (g, j) in mkT:
            e.wait_ge(tT, mkT[(g, j)])

    def wV(e, g, j=0):
        if (g, j) in mkV:
            e.wait_ge(tV, mkV[(g, j)])

    def wS(e, g, j=0):
        if (g, j) in mkS:
            e.wait_ge(tS, mkS[(g, j)])

    xv = x_d.rearrange("(j p) h -> p j h", p=128)
    yv = y_d.rearrange("(j p) h -> p j h", p=128)

    with nc.Block() as block:

        @block.sync
        def _(e):
            e.dma_start(out=stream[:, 0:4, :],
                        in_=xv[:, 0:4, :]).then_inc(sxg[0], 16)
            e.dma_start(out=mask_sb[:, :], in_=mask_d[:, :]).then_inc(swl, 16)
            e.dma_start(out=idb_sb[:, :], in_=idb_d[:, :]).then_inc(swl, 16)
            e.dma_start(out=smalls_sb[:, :], in_=smalls_d[:, :]).then_inc(swl, 16)
            e.dma_start(out=m0t_sb[:, :, :],
                        in_=m0t_d.rearrange("(c p) i -> p c i", p=128)).then_inc(swl, 16)
            e.dma_start(out=stg[:, SOFF:SOFF + 8, :],
                        in_=wt_d.rearrange("(c p) i -> p c i", p=128)).then_inc(swl, 16)
            e.dma_start(out=stg[:, SOFF + 8:SOFF + 12, :].rearrange("p a (u i) -> p (a u) i", u=2),
                        in_=wqkt_d.rearrange("(c p) i -> p c i", p=128)).then_inc(swl, 16)
            e.dma_start(out=stg[:, SOFF + 12:SOFF + 14, :].rearrange("p a (u i) -> p (a u) i", u=4),
                        in_=wvt_d.rearrange("(c p) i -> p c i", p=128)).then_inc(swl, 16)
            e.dma_start(out=stg[:, SOFF + 14:SOFF + 16, :],
                        in_=wot_d.rearrange("(c p) i -> p c i", p=128)).then_inc(swl, 16)
            for g in range(1, NG):
                if stage_in_stream and g == 4:
                    e.wait_ge(suS, SUS_ALL)
                    e.wait_ge(suV, 9)
                e.dma_start(out=stream[:, 4 * g:4 * g + 4, :],
                            in_=xv[:, 4 * g:4 * g + 4, :]).then_inc(sxg[g], 16)
            for j in (0, 1):
                wS(e, "Sa", j)
                e.dma_start_transpose(zf1[j & 1][:, :, :],
                                      zb[j & 3][:, :]).then_inc(stz1[j & 1], 16)
            for j in range(NT):
                r2 = j & 1
                wS(e, "Sc", j)
                wT(e, "Te", j - 2)
                e.dma_start_transpose(zf2[r2][:, 0:4, :],
                                      zb[j & 3][:, 0:512]).then_inc(stz2[r2], 16)
                wS(e, "Sc2", j)
                e.dma_start_transpose(zf2[r2][:, 4:8, :],
                                      zb[j & 3][:, 512:1024]).then_inc(stz2h[r2], 16)
                if j + 2 < NT:
                    wS(e, "Sa", j + 2)
                    wT(e, "Tb", j)
                    e.dma_start_transpose(zf1[j & 1][:, :, :],
                                          zb[(j + 2) & 3][:, :]).then_inc(stz1[j & 1], 16)
                wS(e, "Sf", j)
                wT(e, "Tg", j - 2)
                e.dma_start_transpose(
                    qt_fm[:, :, j * TILE:(j + 1) * TILE],
                    qtb[r2][:, :]).then_inc(stqk[r2], 16)
                wS(e, "Sg", j)
                e.dma_start_transpose(kf[r2][:, :, :],
                                      khb[r2][:, :]).then_inc(stqk[r2], 16)
            wV(e, "Vj", NT - 1)
            e.dma_start(out=cc_in.ap().rearrange("(c p) i -> p c i", p=128),
                        in_=ss_f32[:, :, :]).then_inc(scc, 16)
            e.wait_ge(sccg, 1)
            gview = cc_out.ap().rearrange("(g p) i -> p g i", p=128)
            for cp in range(NP):
                if cp >= 2:
                    wV(e, "VBc", cp - 2)
                e.dma_start(out=gbuf[cp % 2][:, :, :],
                            in_=gview[:, 2 * cp:2 * cp + 2, :]).then_inc(scg[cp % 2], 16)
            wV(e, "VBc", NP - 1)
            for jc in range(MC):
                e.dma_start_transpose(
                    sxt[:, :, jc * 128:(jc + 1) * 128],
                    ssb[0][:, jc, :]).then_inc(stsx, 16)
            for j in range(NT):
                r2 = j & 1
                wS(e, "Ce", j)
                if j < 2:
                    wT(e, "Tb", NT - 2 + j)
                else:
                    wT(e, "Ch", j - 2)
                e.dma_start_transpose(zf1[r2][:, :, :],
                                      zbc[r2][:, :]).then_inc(stz3[r2], 16)
            for g in range(NG):
                wV(e, "Ci", 4 * g + 3)
                e.dma_start(out=yv[:, 4 * g:4 * g + 4, :],
                            in_=stream[:, 4 * g:4 * g + 4, :]).then_inc(sst, 16)
            e.wait_ge(sst, 16 * NG)

        @block.tensor
        def _(e):
            e.wait_ge(suV, 4)
            e.matmul(psum_att[0][:, 0, 0:1], qpat_row[0:1, :], cells[0:1, 14:15],
                     start=True, stop=True)
            e.matmul(psum_att[0][:, 0, 1:2], kpat_row[0:1, :], cells[0:1, 14:15],
                     start=True, stop=True).then_inc(suT, 1)
            e.wait_ge(suV, 5)
            e.wait_ge(suV, 10)
            e.matmul(psum_att[0][:, 2, 0:NT], onesrow[0:1, :], d8row[0:1, :],
                     start=True, stop=True)
            e.matmul(psum_att[0][:, 2, NT:2 * NT], onesrow[0:1, :], dk8row[0:1, :],
                     start=True, stop=True).then_inc(suT, 1)
            e.wait_ge(suV, 7)
            e.wait_ge(suV, 12)
            e.matmul(psum_att[0][:, 1, 0:9], onesrow[0:1, :], w9[0:1, :],
                     start=True, stop=True).then_inc(suT, 1)
            e.wait_ge(suS, 1)
            e.wait_ge(suV, 13)
            e.matmul(psum_att[0][:, 0, 2:3], onesrow[0:1, :], cells[0:1, 13:14],
                     start=True, stop=True).then_inc(suT, 1)
            e.wait_ge(swl, SWL_ALL)
            e.wait_ge(suS, SUS_ALL)
            e.wait_ge(suV, SUV_ALL)

            def Tb(j):
                r = j & 1
                c = j & 1
                e.wait_ge(stz1[r], 16 * (j // 2 + 1))
                wV(e, "Vc2", j - 1)
                wV(e, "Vd1", j - 2)
                for kc in range(HC):
                    for oc in range(2):
                        e.matmul(psum_big[:, oc, :], zf1[r][:, kc, :],
                                 swt[:, kc, oc * 512:(oc + 1) * 512],
                                 start=(kc == 0), stop=(kc == HC - 1))
                    mm = e.matmul(psum_sum[:, c:c + 1], zf1[r][:, kc, :],
                                  s1bf[:, kc:kc + 1],
                                  start=(kc == 0), stop=(kc == HC - 1))
                mm.then_inc(tT, 1)

            def Td(j):
                r = j & 1
                wV(e, "Vi", j - 1)
                e.wait_ge(stz2[r], 16 * (j // 2 + 1))
                for kc in range(4):
                    e.matmul(psum_qkv[:, 0, :], zf2[r][:, kc, :], wqk[:, kc, :],
                             start=(kc == 0), stop=False)
                e.wait_ge(stz2h[r], 16 * (j // 2 + 1))
                for kc in range(4, HC):
                    mm = e.matmul(psum_qkv[:, 0, :], zf2[r][:, kc, :], wqk[:, kc, :],
                                  start=False, stop=(kc == HC - 1))
                mm.then_inc(tT, 1)

            def Te(j):
                r = j & 1
                for kc in range(HC):
                    mm = e.matmul(psum_qkv[:, 1, 0:M], zf2[r][:, kc, :], wv[:, kc, :],
                                  start=(kc == 0), stop=(kc == HC - 1))
                mm.then_inc(tT, 1)

            def Tg(j):
                r = j & 1
                c0 = j * TILE
                e.wait_ge(stqk[r], 32 * (j // 2 + 1))
                wV(e, "Vh", j - 2)
                for c in range(MC):
                    mm = e.matmul(psum_att[r][:, 0, :], kf[r][:, c, :],
                                  qt_fm[:, c, c0:c0 + 128],
                                  start=(c == 0), stop=(c == MC - 1))
                mm.then_inc(tT, 1)

            def Th(j):
                r = j & 1
                c0 = j * TILE
                wV(e, "Vg", j)
                wS(e, "Se", j)
                wV(e, "Vh", j - 2)
                for mi in range(MC):
                    for mj in range(MC):
                        e.matmul(psum_att[r][:, 1 + mi, :],
                                 ssb[r][:, mj, mi * 128:(mi + 1) * 128],
                                 qt_fm[:, mj, c0:c0 + 128],
                                 start=(mi == 0 and mj == 0), stop=False)
                    mm = e.matmul(psum_att[r][:, 1 + mi, :], vvb[r][:, mi * 128:(mi + 1) * 128],
                                  mskT[r][:, :], start=False, stop=(mi == MC - 1))
                mm.then_inc(tT, 1)

            def Ti(j):
                r = j & 1
                wS(e, "Sh", j)
                wV(e, "Vj", j - 1)
                for mk in range(MC):
                    mm = e.matmul(psum_ss[:, mk, :], khb[r][:, mk * 128:(mk + 1) * 128],
                                  vvb[r][:, :], start=(mk == 0), stop=(mk == MC - 1))
                mm.then_inc(tT, 1)

            def Tj(j):
                r = j & 1
                wV(e, "Vh", j)
                wS(e, "Sh", j)
                for kc in range(MC):
                    for oc in range(2):
                        mm = e.matmul(psum_qkv[:, oc, :], rb[r][:, kc, :],
                                      wog[:, kc, oc * 512:(oc + 1) * 512],
                                      start=(kc == 0), stop=(kc == MC - 1))
                mm.then_inc(tT, 1)

            for g, j in T_ORDER:
                if g == "Tb":
                    Tb(j)
                elif g == "Td":
                    Td(j)
                elif g == "Te":
                    Te(j)
                elif g == "Tg":
                    Tg(j)
                elif g == "Th":
                    Th(j)
                elif g == "Ti":
                    Ti(j)
                elif g == "Tj":
                    Tj(j)
                elif g == "TB2":
                    e.wait_ge(stsx, 16 * MC)
                    wV(e, "Vc2", NT - 1)
                    for ic in range(MC):
                        for hc2 in range(2):
                            mm = e.matmul(psum_big[:, hc2, :], sxt[:, ic, 0:128],
                                          wog[:, ic, hc2 * 512:(hc2 + 1) * 512],
                                          start=(ic == 0), stop=(ic == MC - 1))
                    mm.then_inc(tT, 1)
                elif g == "TB3":
                    wV(e, "VB3a", 0)
                    for ic in range(MC):
                        for hc2 in range(2):
                            mm = e.matmul(psum_big[:, hc2, :], sxt[:, ic, 128:256],
                                          wog[:, ic, hc2 * 512:(hc2 + 1) * 512],
                                          start=(ic == 0), stop=(ic == MC - 1))
                    mm.then_inc(tT, 1)
                elif g == "Ca":
                    c0 = j * TILE
                    if j == 0:
                        wV(e, "VB3b", 0)
                        wV(e, "Vi", NT - 1)
                    else:
                        wV(e, "Cb", j - 1)
                    for mjc in range(MC):
                        for hc2 in range(2):
                            mm = e.matmul(psum_qkv[:, hc2, :], qt_fm[:, mjc, c0:c0 + 128],
                                          weff[:, mjc, hc2 * 512:(hc2 + 1) * 512],
                                          start=(mjc == 0), stop=(mjc == MC - 1))
                    mm.then_inc(tT, 1)
                elif g == "Ch":
                    r = j & 1
                    e.wait_ge(stz3[r], 16 * (j // 2 + 1))
                    wV(e, "Ci", j - 1)
                    if j == 0:
                        wV(e, "VB3b", 0)
                    for kc in range(HC):
                        for oc in range(2):
                            mm = e.matmul(psum_big[:, oc, :], zf1[r][:, kc, :],
                                          swt[:, kc, oc * 512:(oc + 1) * 512],
                                          start=(kc == 0), stop=(kc == HC - 1))
                    mm.then_inc(tT, 1)

        @block.vector
        def _(e):
            AXX = AX.X
            e.memset(onesrow[0:1, :], 1.0)
            e.memset(cells[0:1, 14:15], 1.0)
            e.drain().then_inc(suV, 1)
            e.tensor_scalar(cells[0:1, 0:1], cells[0:1, 14:15], DECAY, None, ALU.mult)
            e.drain()
            for i in range(8):
                e.tensor_mul(cells[0:1, i + 1:i + 2], cells[0:1, i:i + 1],
                             cells[0:1, i:i + 1])
                e.drain()
            e.reciprocal(cells[0:1, 9:10], cells[0:1, 0:1])
            e.reciprocal(cells[0:1, 10:11], cells[0:1, 3:4])
            e.reciprocal(cells[0:1, 11:12], cells[0:1, PIDX:PIDX + 1])
            e.drain().then_inc(suV, 1)
            e.memset(qpat_in[0:1, :], 1.0)
            e.drain()
            e.tensor_scalar(qpat_in[0:1, 0:128:16], qpat_in[0:1, 0:128:16],
                            DECAY, None, ALU.mult)
            e.drain()
            e.tensor_tensor_scan(qpat_row[0:1, :], qpat_in[0:1, :], qpat_in[0:1, :],
                                 cells[0:1, 9:10], ALU.mult, ALU.bypass)
            e.drain().then_inc(suV, 1)
            e.reciprocal(kpat_row[0:1, :], qpat_row[0:1, :])
            e.drain()
            e.tensor_scalar(kpat_row[0:1, :], kpat_row[0:1, :],
                            cells[0:1, 9:10], None, ALU.mult)
            e.drain().then_inc(suV, 1)
            e.memset(d8in[0:1, :], 1.0)
            e.drain()
            e.tensor_scalar(d8in[0:1, :], d8in[0:1, :], cells[0:1, 3:4], None, ALU.mult)
            e.drain()
            e.tensor_tensor_scan(d8row[0:1, :], d8in[0:1, :], d8in[0:1, :],
                                 cells[0:1, 10:11], ALU.mult, ALU.bypass)
            e.drain()
            e.reciprocal(dk8row[0:1, :], d8row[0:1, :])
            e.drain().then_inc(suV, 1)
            e.memset(p8in[0:1, :], 1.0)
            e.drain()
            e.tensor_scalar(p8in[0:1, :], p8in[0:1, :], cells[0:1, PIDX:PIDX + 1],
                            None, ALU.mult)
            e.drain()
            e.tensor_tensor_scan(p_row[0:1, :], p8in[0:1, :], p8in[0:1, :],
                                 cells[0:1, 11:12], ALU.mult, ALU.bypass)
            e.drain()
            e.reciprocal(qr_row[0:1, :], p_row[0:1, :])
            e.drain().then_inc(suV, 1)
            e.wait_ge(swl, SWL_ALL)
            e.tensor_mul(t8[0:1, :], smalls_sb[0:1, 0:8], p_row[0:1, :])
            e.drain()
            e.reduce_sum(cells[0:1, 15:16], t8[0:1, :], axis=AXX)
            e.drain()
            e.tensor_mul(w9[0:1, 0:8], qr_row[0:1, :], smalls_sb[0:1, 8:16])
            e.drain()
            e.tensor_scalar(w9[0:1, 0:8], w9[0:1, 0:8], cells[0:1, 15:16], 1.0 / 16.0,
                            ALU.mult, ALU.mult)
            e.tensor_copy(w9[0:1, 8:9], cells[0:1, 15:16])
            e.drain().then_inc(suV, 1)
            e.tensor_copy(flat(wqk),
                          stg[:, SOFF + 8:SOFF + 12, :].rearrange("p a b -> p (a b)"))
            e.drain().then_inc(suV, 1)
            e.tensor_copy(flat(wv),
                          stg[:, SOFF + 12:SOFF + 14, :].rearrange("p a b -> p (a b)"))
            e.drain().then_inc(suV, 1)
            e.wait_ge(suT, 1)
            e.tensor_copy(qbase[:, :], psum_att[0][:, 0, 0:1])
            e.tensor_copy(kbase[:, :], psum_att[0][:, 0, 1:2])
            e.drain().then_inc(suV, 1)
            e.wait_ge(suT, 2)
            e.tensor_scalar(qscale[:, :], psum_att[0][:, 2, 0:NT], qbase[:, :],
                            None, ALU.mult)
            e.drain().then_inc(suV, 1)
            e.tensor_scalar(kscale[:, :], psum_att[0][:, 2, NT:2 * NT], kbase[:, :],
                            None, ALU.mult)
            e.drain().then_inc(suV, 1)
            e.wait_ge(suT, 3)
            e.tensor_copy(wcols[:, :], psum_att[0][:, 1, 0:9])
            e.drain().then_inc(suV, 1)
            e.wait_ge(suT, 4)
            e.tensor_copy(gate_col[:, :], psum_att[0][:, 0, 2:3])
            e.drain().then_inc(suV, 1)
            e.wait_ge(suS, 2)
            for kc in range(HC):
                e.reduce_sum(s1f32[:, kc:kc + 1], swt[:, kc, :], axis=AXX)
            e.drain()
            e.tensor_copy(s1bf[:, :], s1f32[:, :])
            e.drain().then_inc(suV, 1)

            def Va1(j):
                r = j & 3
                e.wait_ge(sxg[j // 4], 16)
                wS(e, "Sc", j - 4)
                e.reduce_sum(sumc[r], stream[:, j, :], axis=AXX)
                wS(e, "Sb", j)
                e.drain()
                e.tensor_scalar(mneg[r], sumc[r], -1.0 / H, None, ALU.mult)
                e.tensor_scalar(ve[r], sumsq[r], 1.0 / H, EPS, ALU.mult, ALU.add)
                e.drain()
                e.tensor_mul(msq[r], mneg[r], mneg[r])
                e.drain()
                e.tensor_sub(ve[r], ve[r], msq[r]).then_inc(tV, 1)

            def Va2(j):
                r = j & 3
                wS(e, "Sq1", j)
                e.reciprocal(inv2c[r], s0c[r])
                e.drain()
                e.tensor_scalar(invh[r], inv2c[r], 0.5, None, ALU.mult).then_inc(tV, 1)

            def Vc(j):
                r = j & 3
                wT(e, "Tb", j)
                e.scalar_tensor_tensor(stream[:, j, 0:512], psum_big[:, 0, :],
                                       invh[r], stream[:, j, 0:512],
                                       ALU.mult, ALU.add).then_inc(tV, 1)

            def Vc2(j):
                r = j & 3
                e.scalar_tensor_tensor(stream[:, j, 512:1024], psum_big[:, 1, :],
                                       invh[r], stream[:, j, 512:1024],
                                       ALU.mult, ALU.add).then_inc(tV, 1)

            def Vd1(j):
                r = j & 3
                wT(e, "Tb", j)
                e.tensor_scalar(tmpc[r], psum_sum[:, (j & 1):(j & 1) + 1],
                                invh[r], -1.0 / H, ALU.mult, ALU.mult)
                e.drain()
                e.scalar_tensor_tensor(mneg[r], sumc[r], -1.0 / H, tmpc[r],
                                       ALU.mult, ALU.add).then_inc(tV, 1)

            def Vw(j):
                r = j & 3
                wS(e, "Sd", j)
                e.tensor_scalar(ve[r], sumsq[r], 1.0 / H, EPS, ALU.mult, ALU.add)
                e.drain()
                e.tensor_mul(msq[r], mneg[r], mneg[r])
                e.drain()
                e.tensor_sub(ve[r], ve[r], msq[r]).then_inc(tV, 1)

            def Vd2(j):
                r = j & 3
                wS(e, "Sq2", j)
                e.reciprocal(inv2c[r], s0c[r])
                e.drain()
                e.tensor_scalar(qcol[r], qscale[:, j:j + 1], inv2c[r], None, ALU.mult)
                e.tensor_scalar(kcol[r], kscale[:, j:j + 1], inv2c[r],
                                None, ALU.mult).then_inc(tV, 1)

            def Vg(j):
                r = j & 1
                wT(e, "Tg", j)
                e.tensor_mul(mskT[r][:, :], psum_att[r][:, 0, :],
                             mask_sb[:, :]).then_inc(tV, 1)

            def Vh(j):
                r = j & 1
                wT(e, "Th", j)
                e.tensor_copy(flat(rb[r]),
                              psum_att[r][:, 1:3, :].rearrange("p a b -> p (a b)")
                              ).then_inc(tV, 1)

            def Vi(j):
                wT(e, "Tj", j)
                e.tensor_add(stream[:, j, :], stream[:, j, :],
                             flat(psum_qkv)).then_inc(tV, 1)

            def Vj(j):
                wT(e, "Ti", j)
                if j == 0:
                    e.tensor_copy(flat(ss_f32), flat(psum_ss)).then_inc(tV, 1)
                else:
                    e.tensor_add(flat(ss_f32), flat(ss_f32),
                                 flat(psum_ss)).then_inc(tV, 1)

            for g, j in V_ORDER:
                if g == "Va1":
                    Va1(j)
                elif g == "Va2":
                    Va2(j)
                elif g == "Vc":
                    Vc(j)
                elif g == "Vc2":
                    Vc2(j)
                elif g == "Vd1":
                    Vd1(j)
                elif g == "Vw":
                    Vw(j)
                elif g == "Vd2":
                    Vd2(j)
                elif g == "Vg":
                    Vg(j)
                elif g == "Vh":
                    Vh(j)
                elif g == "Vi":
                    Vi(j)
                elif g == "Vj":
                    Vj(j)
                elif g == "VBc":
                    cp = j
                    e.wait_ge(scg[cp % 2], 16 * (cp // 2 + 1))
                    if cp == 0:
                        e.tensor_scalar(flat(ss_ext), flat(m0t_sb),
                                        wcols[:, 8:9], None, ALU.mult)
                        e.drain()
                    e.scalar_tensor_tensor(flat(ss_ext), flat(gbuf[cp % 2]),
                                           wcols[:, cp:cp + 1], flat(ss_ext),
                                           ALU.mult, ALU.add)
                    e.drain()
                    if cp == NP - 1:
                        wT(e, "Th", NT - 1)
                        e.tensor_copy(flat(ssb[0]), flat(ss_ext)).then_inc(tV, 1)
                    else:
                        e.nop()
                        e.drain().then_inc(tV, 1)
                elif g == "VB3a":
                    wT(e, "TB2", 0)
                    e.tensor_copy(weff[:, 0, :], flat(psum_big)).then_inc(tV, 1)
                elif g == "VB3b":
                    wT(e, "TB3", 0)
                    e.tensor_copy(weff[:, 1, :], flat(psum_big)).then_inc(tV, 1)
                elif g == "Cb":
                    wT(e, "Ca", j)
                    e.drain()
                    e.tensor_add(stream[:, j, :], stream[:, j, :],
                                 flat(psum_qkv)).then_inc(tV, 1)
                elif g == "Cc1":
                    r = j & 3
                    wS(e, "Ce", j - 2)
                    e.drain()
                    e.reduce_sum(sumc[r], stream[:, j, :], axis=AXX)
                    e.drain()
                    e.tensor_scalar(mneg[r], sumc[r], -1.0 / H,
                                    None, ALU.mult).then_inc(tV, 1)
                elif g == "Cw":
                    r = j & 3
                    wS(e, "Cd", j)
                    e.tensor_scalar(ve[r], sumsq[r], 1.0 / H, EPS, ALU.mult, ALU.add)
                    e.drain()
                    e.tensor_mul(msq[r], mneg[r], mneg[r])
                    e.drain()
                    e.tensor_sub(ve[r], ve[r], msq[r]).then_inc(tV, 1)
                elif g == "Cc2":
                    r = j & 3
                    wS(e, "Cq", j)
                    e.reciprocal(inv2c[r], s0c[r])
                    e.drain()
                    e.tensor_scalar(invh[r], inv2c[r], 0.5,
                                    None, ALU.mult).then_inc(tV, 1)
                elif g == "Ci":
                    r = j & 3
                    wT(e, "Ch", j)
                    e.scalar_tensor_tensor(stream[:, j, :], flat(psum_big),
                                           invh[r], stream[:, j, :],
                                           ALU.mult, ALU.add).then_inc(tV, 1)

        @block.scalar
        def _(e):
            e.wait_ge(swl, SWL_ALL)
            e.activation(cells[0:1, 13:14], smalls_sb[0:1, 16:17],
                         ACT.Sigmoid).then_inc(suS, 1)
            e.activation(flat(swt), stg[:, SOFF:SOFF + 8, :].rearrange("p a b -> p (a b)"),
                         ACT.Sign).then_inc(suS, 1)
            e.wait_ge(suV, SUV_ALL)
            for c in range(MC):
                aa = e.activation(wog[:, c, :], stg[:, SOFF + 14 + c, :], ACT.Copy,
                                  scale=gate_col[:, :])
            aa.then_inc(suS, 1)

            def Sb(j):
                r = j & 3
                e.wait_ge(sxg[j // 4], 16)
                if j >= 4:
                    e.wait_ge(stz2[j & 1], 16 * ((j - 4) // 2 + 1))
                    e.wait_ge(stz2h[j & 1], 16 * ((j - 4) // 2 + 1))
                wV(e, "Vd1", j - 4)
                e.activation(zb[r][:, :], stream[:, j, :], ACT.Square,
                             accum_out=sumsq[r]).then_inc(tS, 1)

            def Sq1(j):
                r = j & 3
                wV(e, "Va1", j)
                wV(e, "Va2", j - 4)
                e.activation(s0c[r], ve[r], ACT.Sqrt).then_inc(tS, 1)

            def Sa(j):
                r = j & 3
                wV(e, "Va1", j)
                e.activation(zb[r][:, :], stream[:, j, :], ACT.Identity,
                             bias=mneg[r], scale=1.0).then_inc(tS, 1)

            def Sd(j):
                r = j & 3
                wV(e, "Vc2", j)
                e.activation(zbc[j & 1][:, :], stream[:, j, :], ACT.Square,
                             accum_out=sumsq[r]).then_inc(tS, 1)

            def Sq2(j):
                r = j & 3
                wV(e, "Vw", j)
                wV(e, "Va2", j)
                e.activation(s0c[r], ve[r], ACT.Sqrt).then_inc(tS, 1)

            def Sc(j):
                r = j & 3
                wV(e, "Vc", j)
                e.wait_ge(stz1[j & 1], 16 * (j // 2 + 1))
                e.activation(zb[r][:, 0:512], stream[:, j, 0:512], ACT.Identity,
                             bias=mneg[r], scale=1.0).then_inc(tS, 1)

            def Sc2(j):
                r = j & 3
                wV(e, "Vc2", j)
                e.activation(zb[r][:, 512:1024], stream[:, j, 512:1024],
                             ACT.Identity,
                             bias=mneg[r], scale=1.0).then_inc(tS, 1)

            def Se(j):
                r = j & 1
                if j == 0:
                    e.memzero(flat(ssb[0])).then_inc(tS, 1)
                else:
                    wV(e, "Vj", j - 1)
                    wT(e, "Th", j - 2)
                    e.activation(flat(ssb[r]), flat(ss_f32), ACT.Copy,
                                 scale=1.0 / 16.0).then_inc(tS, 1)

            def Sf(j):
                r = j & 1
                wT(e, "Td", j)
                wV(e, "Vd2", j)
                if j >= 2:
                    e.wait_ge(stqk[r], 32 * ((j - 2) // 2 + 1))
                e.activation(qtb[r][:, :], psum_qkv[:, 0, 0:M], ACT.Copy,
                             scale=qcol[j & 3]).then_inc(tS, 1)

            def Sg(j):
                r = j & 1
                e.activation(khb[r][:, :], psum_qkv[:, 0, M:2 * M], ACT.Copy,
                             scale=kcol[j & 3]).then_inc(tS, 1)

            def Sh(j):
                r = j & 1
                wT(e, "Te", j)
                wT(e, "Ti", j - 2)
                e.activation(vvb[r][:, :], psum_qkv[:, 1, 0:M], ACT.Copy,
                             scale=inv2c[j & 3]).then_inc(tS, 1)

            for g, j in S_ORDER:
                if g == "Sb":
                    Sb(j)
                elif g == "Sc2":
                    Sc2(j)
                elif g == "Sq1":
                    Sq1(j)
                elif g == "Sa":
                    Sa(j)
                elif g == "Sd":
                    Sd(j)
                elif g == "Sq2":
                    Sq2(j)
                elif g == "Sc":
                    Sc(j)
                elif g == "Se":
                    Se(j)
                elif g == "Sf":
                    Sf(j)
                elif g == "Sg":
                    Sg(j)
                elif g == "Sh":
                    Sh(j)
                elif g == "Cd":
                    r = j & 1
                    r4 = j & 3
                    wV(e, "Cb", j)
                    if j >= 2:
                        e.wait_ge(stz3[r], 16 * ((j - 2) // 2 + 1))
                    wV(e, "Cc1", j - 2)
                    e.activation(zbc[r][:, :], stream[:, j, :], ACT.Square,
                                 accum_out=sumsq[r4]).then_inc(tS, 1)
                elif g == "Cq":
                    r = j & 3
                    wV(e, "Cw", j)
                    wV(e, "Cc2", j - 2)
                    e.activation(s0c[r], ve[r], ACT.Sqrt).then_inc(tS, 1)
                elif g == "Ce":
                    r = j & 1
                    wV(e, "Cc1", j)
                    e.drain()
                    e.activation(zbc[r][:, :], stream[:, j, :], ACT.Identity,
                                 bias=mneg[j & 3], scale=1.0).then_inc(tS, 1)

        @block.gpsimd
        def _(e):
            e.wait_ge(scc, 16)
            e.collective_compute(
                "AllGather", ALU.bypass,
                replica_groups=[list(range(NCORES))],
                ins=[cc_in.ap().opt()],
                outs=[cc_out.ap().opt()],
            ).then_inc(sccg, 1)

    es.close()
    return nc


def _host_layout(x, Wq, Wk, Wv, Wo, gate, M0, W, NT=32, NCORES=8):
    TPC = NT * TILE // B
    xs = []
    for c in range(NCORES):
        xc = np.ascontiguousarray(
            x[:, c * TPC:(c + 1) * TPC, :].transpose(1, 0, 2)).reshape(NT * TILE, H)
        xs.append(xc)
    wt = np.ascontiguousarray(W.T)
    wqkt = np.ascontiguousarray(np.concatenate([Wq.T, Wk.T], axis=1))
    wvt = np.ascontiguousarray(Wv.T)
    wot = np.ascontiguousarray(Wo.T)
    m0t = np.ascontiguousarray(M0.T)
    wtok = np.arange(TILE) // B
    mask = ((wtok[:, None] < wtok[None, :]).astype(np.float32) / np.float32(16.0))
    idb = np.eye(128).astype(ml_dtypes.bfloat16)
    gate_v = float(np.asarray(gate, np.float32).reshape(-1)[0])
    maps = []
    for c in range(NCORES):
        smalls = np.zeros((1, 128), np.float32)
        smalls[0, c] = 1.0
        smalls[0, 8:8 + c] = 1.0
        smalls[0, 16] = gate_v
        maps.append({
            "x": xs[c], "wt": wt, "wqkt": wqkt, "wvt": wvt, "wot": wot,
            "m0t": m0t, "mask": mask, "idb": idb, "smalls": smalls,
        })
    return maps


def kernel(L0_attn_x, L0_attn_Wq, L0_attn_Wk, L0_attn_Wv, L0_attn_Wo,
           L0_attn_gate, L0_attn_M0, L0_ffn_weight):
    NT, NCORES = 32, 8
    x = np.asarray(L0_attn_x, np.float32)
    maps = _host_layout(x, np.asarray(L0_attn_Wq, np.float32),
                        np.asarray(L0_attn_Wk, np.float32),
                        np.asarray(L0_attn_Wv, np.float32),
                        np.asarray(L0_attn_Wo, np.float32),
                        np.asarray(L0_attn_gate, np.float32),
                        np.asarray(L0_attn_M0, np.float32),
                        np.asarray(L0_ffn_weight, np.float32), NT, NCORES)
    nc = build_nc(NT, NCORES)
    res = run_bass_kernel_spmd(nc, maps, core_ids=list(range(NCORES)))
    Bv, T = x.shape[0], x.shape[1]
    TPC = T // NCORES
    out = np.empty((Bv, T, H), np.float32)
    for c in range(NCORES):
        yc = np.asarray(res.results[c]["y"]).reshape(TPC, B, H)
        out[:, c * TPC:(c + 1) * TPC, :] = yc.transpose(1, 0, 2)
    return out


# version 21
# speedup vs baseline: 1.0070x; 1.0006x over previous
import numpy as np
import ml_dtypes
import concourse.bass as bass
import concourse.mybir as mybir
from concourse.bass_utils import run_bass_kernel_spmd

F32 = mybir.dt.float32
BF16 = mybir.dt.bfloat16
AX = mybir.AxisListType
ALU = mybir.AluOpType
ACT = mybir.ActivationFunctionType

H = 1024
M = 256
B = 16
TILE = 128
HC = H // 128
MC = M // 128
DECAY = 0.99
EPS = 1e-5


def build_nc(NT=32, NCORES=8):
    NG = NT // 4
    nc = bass.Bass("TRN2", target_bir_lowering=False, debug=False,
                   dynamic_dma_scratch_size=512)

    x_d = nc.dram_tensor("x", [NT * TILE, H], F32, kind="ExternalInput").ap()
    wt_d = nc.dram_tensor("wt", [H, H], F32, kind="ExternalInput").ap()
    wqkt_d = nc.dram_tensor("wqkt", [H, 2 * M], F32, kind="ExternalInput").ap()
    wvt_d = nc.dram_tensor("wvt", [H, M], F32, kind="ExternalInput").ap()
    wot_d = nc.dram_tensor("wot", [M, H], F32, kind="ExternalInput").ap()
    m0t_d = nc.dram_tensor("m0t", [M, M], F32, kind="ExternalInput").ap()
    mask_d = nc.dram_tensor("mask", [128, 128], F32, kind="ExternalInput").ap()
    idb_d = nc.dram_tensor("idb", [128, 128], BF16, kind="ExternalInput").ap()
    smalls_d = nc.dram_tensor("smalls", [1, 128], F32, kind="ExternalInput").ap()
    y_d = nc.dram_tensor("y", [NT * TILE, H], F32, kind="ExternalOutput").ap()
    cc_in = nc.dram_tensor("cc_in", [M, M], F32)
    cc_out = nc.dram_tensor("cc_out", [NCORES * M, M], F32,
                            addr_space="Shared" if NCORES > 4 else "Local")

    import contextlib
    es = contextlib.ExitStack()
    sb = lambda name, shape, dt: es.enter_context(nc.sbuf_tensor(name, shape, dt))
    psum = lambda name, shape, dt=F32: es.enter_context(nc.psum_tensor(name, shape, dt))
    sem = lambda name: es.enter_context(nc.semaphore(name))

    stream = sb("stream", [128, NT, H], F32)
    if NT >= 32:
        stg, SOFF, stage_in_stream = stream, 16, True
    else:
        stg, SOFF = sb("wstage", [128, 16, H], F32), 0
        stage_in_stream = False
    qt_fm = sb("qt_fm", [128, MC, NT * TILE], BF16)
    swt = sb("swt", [128, HC, H], BF16)
    wqk = sb("wqk", [128, HC, 2 * M], BF16)
    wv = sb("wv", [128, HC, M], BF16)
    wog = sb("wog", [128, MC, H], BF16)
    weff = sb("weff", [128, MC, H], BF16)
    m0t_sb = sb("m0t_sb", [128, MC, M], F32)
    gbuf = [sb(f"gbuf{i}", [128, MC, M], F32) for i in range(2)]
    zb = [sb(f"zb{i}", [128, H], BF16) for i in range(4)]
    zbc = [sb(f"zbc{i}", [128, H], BF16) for i in range(2)]
    zf1 = [sb(f"zf1_{i}", [128, HC, 128], BF16) for i in range(2)]
    zf2 = [sb(f"zf2_{i}", [128, HC, 128], BF16) for i in range(2)]
    qtb = [sb(f"qtb{i}", [128, M], BF16) for i in range(2)]
    khb = [sb(f"khb{i}", [128, M], BF16) for i in range(2)]
    vvb = [sb(f"vvb{i}", [128, M], BF16) for i in range(2)]
    kf = [sb(f"kf{i}", [128, MC, 128], BF16) for i in range(2)]
    mskT = [sb(f"mskT{i}", [128, 128], BF16) for i in range(2)]
    ssb = [sb(f"ssb{i}", [128, MC, M], BF16) for i in range(2)]
    rb = [sb(f"rb{i}", [128, MC, 128], BF16) for i in range(2)]
    ss_f32 = sb("ss_f32", [128, MC, M], F32)
    ss_ext = sb("ss_ext", [128, MC, M], F32)
    sxt = sb("sxt", [128, MC, M], BF16)
    mask_sb = sb("mask_sb", [128, 128], F32)
    idb_sb = sb("idb_sb", [128, 128], BF16)
    smalls_sb = sb("smalls_sb", [1, 128], F32)
    gate_col = sb("gate_col", [128, 1], F32)
    qscale = sb("qscale", [128, NT], F32)
    kscale = sb("kscale", [128, NT], F32)
    wcols = sb("wcols", [128, 9], F32)
    qbase = sb("qbase", [128, 1], F32)
    kbase = sb("kbase", [128, 1], F32)
    onesrow = sb("onesrow", [1, 128], F32)
    cells = sb("cells", [1, 16], F32)
    qpat_in = sb("qpat_in", [1, 128], F32)
    qpat_row = sb("qpat_row", [1, 128], F32)
    kpat_row = sb("kpat_row", [1, 128], F32)
    d8in = sb("d8in", [1, NT], F32)
    d8row = sb("d8row", [1, NT], F32)
    dk8row = sb("dk8row", [1, NT], F32)
    p8in = sb("p8in", [1, 8], F32)
    p_row = sb("p_row", [1, 8], F32)
    qr_row = sb("qr_row", [1, 8], F32)
    t8 = sb("t8", [1, 8], F32)
    w9 = sb("w9", [1, 9], F32)
    statc = [sb(f"statc{i}", [128, 16], F32) for i in range(4)]
    s1f32 = sb("s1f32", [128, HC], F32)
    s1bf = sb("s1bf", [128, HC], BF16)
    mnegall = sb("mnegall", [128, NT], F32)
    wecol = sb("wecol", [128, MC], F32)
    wecol_bf = sb("wecol_bf", [128, MC], BF16)

    def _col(idx):
        return [statc[k][:, idx:idx + 1] for k in range(4)]
    sumc, sumsq, mneg, ve, msq, s0c = (_col(0), _col(1), _col(2), _col(3),
                                       _col(4), _col(5))
    r0c, tc_, invh, inv2c, qcol, kcol = (_col(6), _col(7), _col(8), _col(9),
                                         _col(10), _col(11))
    sump, tmpc = _col(12), _col(13)

    psum_ss = psum("psum_ss", [128, MC, M])
    psum_qkv = psum("psum_qkv", [128, 2, 512])
    psum_big = psum("psum_big", [128, 2, 512])
    psum_att = [psum(f"psum_att{i}", [128, 3, 128]) for i in range(2)]
    psum_sum = psum("psum_sum", [128, 2])
    flat = lambda t: t.ap().rearrange("p a b -> p (a b)")

    swl = sem("swl")
    sxg = [sem(f"sxg{g}") for g in range(NG)]
    sst = sem("sst")
    scc = sem("scc")
    sccg = sem("sccg")
    scg = [sem("scg0"), sem("scg1")]
    suT, suV, suS = sem("suT"), sem("suV"), sem("suS")
    stz1 = [sem("stz1a"), sem("stz1b")]
    stz2 = [sem("stz2a"), sem("stz2b")]
    stz2h = [sem("stz2ha"), sem("stz2hb")]
    stz3 = [sem("stz3a"), sem("stz3b")]
    stqk = [sem("stqka"), sem("stqkb")]
    stsx = sem("stsx")
    tT, tV, tS = sem("tT"), sem("tV"), sem("tS")

    SWL_ALL = 8 * 16
    SUV_ALL = 15
    SUS_ALL = 3
    NP = NCORES
    PIDX = 3 + int(np.log2(NT))
    assert 2 ** (PIDX - 3) == NT, "NT must be a power of two"

    def _ord(body, lo, hi, prologue=(), epilogue=()):
        out = list(prologue)
        for j in range(lo, hi):
            for g, dj in body:
                jj = j + dj
                if 0 <= jj < NT:
                    out.append((g, jj))
        out += list(epilogue)
        return out

    T_ORDER = _ord([("Tb", 0), ("Te", -1), ("Tg", -1), ("Th", -1),
                    ("Ti", -1), ("Tj", -1), ("Td", 0)],
                   -1, NT + 1)
    T_ORDER += [("TB2", 0), ("TB3", 0)]
    T_ORDER += _ord([("Ca", 0), ("Ch", -2)], 0, NT + 2)

    V_ORDER = _ord([("Vd1", 0), ("Vc", 0), ("Vc2", 0), ("Vg", -1), ("Vh", -1),
                    ("Vw", 0), ("Vi", -1), ("Vj", -1), ("Vd2", 0),
                    ("Va1", 2), ("Va2", 2)],
                   -2, NT + 1)
    for cp in range(NP):
        V_ORDER.append(("VBc", cp))
    V_ORDER += [("VB3a", 0), ("VB3b", 0)]
    V_ORDER += _ord([("Cb", 0), ("Cc1", 0), ("Cw", 0), ("Cc2", 0),
                     ("Ci", -2)],
                    0, NT + 2)

    S_ORDER = _ord([("Sf", -1), ("Sg", -1), ("Sh", -1), ("Se", -1),
                    ("Sc", 0), ("Sc2", 0), ("Sd", 0), ("Sq2", 0), ("Sb", 2),
                    ("Sq1", 2), ("Sa", 2)],
                   -2, NT + 1)
    S_ORDER += _ord([("Cd", 0), ("Ce", 0), ("Cq", 0)], 0, NT)
    assert len(T_ORDER) == len(set(T_ORDER))
    assert len(V_ORDER) == len(set(V_ORDER))
    assert len(S_ORDER) == len(set(S_ORDER))

    mkT = {g: i + 1 for i, g in enumerate(T_ORDER)}
    mkV = {g: i + 1 for i, g in enumerate(V_ORDER)}
    mkS = {g: i + 1 for i, g in enumerate(S_ORDER)}

    def wT(e, g, j=0):
        if (g, j) in mkT:
            e.wait_ge(tT, mkT[(g, j)])

    def wV(e, g, j=0):
        if (g, j) in mkV:
            e.wait_ge(tV, mkV[(g, j)])

    def wS(e, g, j=0):
        if (g, j) in mkS:
            e.wait_ge(tS, mkS[(g, j)])

    xv = x_d.rearrange("(j p) h -> p j h", p=128)
    yv = y_d.rearrange("(j p) h -> p j h", p=128)

    with nc.Block() as block:

        @block.sync
        def _(e):
            e.dma_start(out=stream[:, 0:4, :],
                        in_=xv[:, 0:4, :]).then_inc(sxg[0], 16)
            e.dma_start(out=mask_sb[:, :], in_=mask_d[:, :]).then_inc(swl, 16)
            e.dma_start(out=idb_sb[:, :], in_=idb_d[:, :]).then_inc(swl, 16)
            e.dma_start(out=smalls_sb[:, :], in_=smalls_d[:, :]).then_inc(swl, 16)
            e.dma_start(out=m0t_sb[:, :, :],
                        in_=m0t_d.rearrange("(c p) i -> p c i", p=128)).then_inc(swl, 16)
            e.dma_start(out=stg[:, SOFF:SOFF + 8, :],
                        in_=wt_d.rearrange("(c p) i -> p c i", p=128)).then_inc(swl, 16)
            e.dma_start(out=stg[:, SOFF + 8:SOFF + 12, :].rearrange("p a (u i) -> p (a u) i", u=2),
                        in_=wqkt_d.rearrange("(c p) i -> p c i", p=128)).then_inc(swl, 16)
            e.dma_start(out=stg[:, SOFF + 12:SOFF + 14, :].rearrange("p a (u i) -> p (a u) i", u=4),
                        in_=wvt_d.rearrange("(c p) i -> p c i", p=128)).then_inc(swl, 16)
            e.dma_start(out=stg[:, SOFF + 14:SOFF + 16, :],
                        in_=wot_d.rearrange("(c p) i -> p c i", p=128)).then_inc(swl, 16)
            for g in range(1, NG):
                if stage_in_stream and g == 4:
                    e.wait_ge(suS, SUS_ALL)
                    e.wait_ge(suV, 9)
                e.dma_start(out=stream[:, 4 * g:4 * g + 4, :],
                            in_=xv[:, 4 * g:4 * g + 4, :]).then_inc(sxg[g], 16)
            for j in (0, 1):
                wS(e, "Sa", j)
                e.dma_start_transpose(zf1[j & 1][:, :, :],
                                      zb[j & 3][:, :]).then_inc(stz1[j & 1], 16)
            for j in range(NT):
                r2 = j & 1
                wS(e, "Sc", j)
                wT(e, "Te", j - 2)
                e.dma_start_transpose(zf2[r2][:, 0:4, :],
                                      zb[j & 3][:, 0:512]).then_inc(stz2[r2], 16)
                wS(e, "Sc2", j)
                e.dma_start_transpose(zf2[r2][:, 4:8, :],
                                      zb[j & 3][:, 512:1024]).then_inc(stz2h[r2], 16)
                if j + 2 < NT:
                    wS(e, "Sa", j + 2)
                    wT(e, "Tb", j)
                    e.dma_start_transpose(zf1[j & 1][:, :, :],
                                          zb[(j + 2) & 3][:, :]).then_inc(stz1[j & 1], 16)
                wS(e, "Sf", j)
                wT(e, "Tg", j - 2)
                e.dma_start_transpose(
                    qt_fm[:, :, j * TILE:(j + 1) * TILE],
                    qtb[r2][:, :]).then_inc(stqk[r2], 16)
                wS(e, "Sg", j)
                e.dma_start_transpose(kf[r2][:, :, :],
                                      khb[r2][:, :]).then_inc(stqk[r2], 16)
            wV(e, "Vj", NT - 1)
            e.dma_start(out=cc_in.ap().rearrange("(c p) i -> p c i", p=128),
                        in_=ss_f32[:, :, :]).then_inc(scc, 16)
            e.wait_ge(sccg, 1)
            gview = cc_out.ap().rearrange("(g p) i -> p g i", p=128)
            for cp in range(NP):
                if cp >= 2:
                    wV(e, "VBc", cp - 2)
                e.dma_start(out=gbuf[cp % 2][:, :, :],
                            in_=gview[:, 2 * cp:2 * cp + 2, :]).then_inc(scg[cp % 2], 16)
            wV(e, "VBc", NP - 1)
            for jc in range(MC):
                e.dma_start_transpose(
                    sxt[:, :, jc * 128:(jc + 1) * 128],
                    ssb[0][:, jc, :]).then_inc(stsx, 16)
            for j in range(NT):
                r2 = j & 1
                wS(e, "Ce", j)
                if j < 2:
                    wT(e, "Tb", NT - 2 + j)
                else:
                    wT(e, "Ch", j - 2)
                e.dma_start_transpose(zf1[r2][:, :, :],
                                      zbc[r2][:, :]).then_inc(stz3[r2], 16)
            for g in range(NG):
                wV(e, "Ci", 4 * g + 3)
                e.dma_start(out=yv[:, 4 * g:4 * g + 4, :],
                            in_=stream[:, 4 * g:4 * g + 4, :]).then_inc(sst, 16)
            e.wait_ge(sst, 16 * NG)

        @block.tensor
        def _(e):
            e.wait_ge(suV, 4)
            e.matmul(psum_att[0][:, 0, 0:1], qpat_row[0:1, :], cells[0:1, 14:15],
                     start=True, stop=True)
            e.matmul(psum_att[0][:, 0, 1:2], kpat_row[0:1, :], cells[0:1, 14:15],
                     start=True, stop=True).then_inc(suT, 1)
            e.wait_ge(suV, 5)
            e.wait_ge(suV, 10)
            e.matmul(psum_att[0][:, 2, 0:NT], onesrow[0:1, :], d8row[0:1, :],
                     start=True, stop=True)
            e.matmul(psum_att[0][:, 2, NT:2 * NT], onesrow[0:1, :], dk8row[0:1, :],
                     start=True, stop=True).then_inc(suT, 1)
            e.wait_ge(suV, 7)
            e.wait_ge(suV, 12)
            e.matmul(psum_att[0][:, 1, 0:9], onesrow[0:1, :], w9[0:1, :],
                     start=True, stop=True).then_inc(suT, 1)
            e.wait_ge(suS, 1)
            e.wait_ge(suV, 13)
            e.matmul(psum_att[0][:, 0, 2:3], onesrow[0:1, :], cells[0:1, 13:14],
                     start=True, stop=True).then_inc(suT, 1)
            e.wait_ge(swl, SWL_ALL)
            e.wait_ge(suS, SUS_ALL)
            e.wait_ge(suV, SUV_ALL)

            def Tb(j):
                r = j & 1
                c = j & 1
                e.wait_ge(stz1[r], 16 * (j // 2 + 1))
                wV(e, "Vc2", j - 1)
                wV(e, "Vd1", j - 2)
                for kc in range(HC):
                    for oc in range(2):
                        e.matmul(psum_big[:, oc, :], zf1[r][:, kc, :],
                                 swt[:, kc, oc * 512:(oc + 1) * 512],
                                 start=(kc == 0), stop=(kc == HC - 1))
                    mm = e.matmul(psum_sum[:, c:c + 1], zf1[r][:, kc, :],
                                  s1bf[:, kc:kc + 1],
                                  start=(kc == 0), stop=(kc == HC - 1))
                mm.then_inc(tT, 1)

            def Td(j):
                r = j & 1
                wV(e, "Vi", j - 1)
                e.wait_ge(stz2[r], 16 * (j // 2 + 1))
                for kc in range(4):
                    e.matmul(psum_qkv[:, 0, :], zf2[r][:, kc, :], wqk[:, kc, :],
                             start=(kc == 0), stop=False)
                e.wait_ge(stz2h[r], 16 * (j // 2 + 1))
                for kc in range(4, HC):
                    mm = e.matmul(psum_qkv[:, 0, :], zf2[r][:, kc, :], wqk[:, kc, :],
                                  start=False, stop=(kc == HC - 1))
                mm.then_inc(tT, 1)

            def Te(j):
                r = j & 1
                for kc in range(HC):
                    mm = e.matmul(psum_qkv[:, 1, 0:M], zf2[r][:, kc, :], wv[:, kc, :],
                                  start=(kc == 0), stop=(kc == HC - 1))
                mm.then_inc(tT, 1)

            def Tg(j):
                r = j & 1
                c0 = j * TILE
                e.wait_ge(stqk[r], 32 * (j // 2 + 1))
                wV(e, "Vh", j - 2)
                for c in range(MC):
                    mm = e.matmul(psum_att[r][:, 0, :], kf[r][:, c, :],
                                  qt_fm[:, c, c0:c0 + 128],
                                  start=(c == 0), stop=(c == MC - 1))
                mm.then_inc(tT, 1)

            def Th(j):
                r = j & 1
                c0 = j * TILE
                wV(e, "Vg", j)
                wS(e, "Se", j)
                wV(e, "Vh", j - 2)
                for mi in range(MC):
                    for mj in range(MC):
                        e.matmul(psum_att[r][:, 1 + mi, :],
                                 ssb[r][:, mj, mi * 128:(mi + 1) * 128],
                                 qt_fm[:, mj, c0:c0 + 128],
                                 start=(mi == 0 and mj == 0), stop=False)
                    mm = e.matmul(psum_att[r][:, 1 + mi, :], vvb[r][:, mi * 128:(mi + 1) * 128],
                                  mskT[r][:, :], start=False, stop=(mi == MC - 1))
                mm.then_inc(tT, 1)

            def Ti(j):
                r = j & 1
                wS(e, "Sh", j)
                wV(e, "Vj", j - 1)
                for mk in range(MC):
                    mm = e.matmul(psum_ss[:, mk, :], khb[r][:, mk * 128:(mk + 1) * 128],
                                  vvb[r][:, :], start=(mk == 0), stop=(mk == MC - 1))
                mm.then_inc(tT, 1)

            def Tj(j):
                r = j & 1
                wV(e, "Vh", j)
                wS(e, "Sh", j)
                for kc in range(MC):
                    for oc in range(2):
                        mm = e.matmul(psum_qkv[:, oc, :], rb[r][:, kc, :],
                                      wog[:, kc, oc * 512:(oc + 1) * 512],
                                      start=(kc == 0), stop=(kc == MC - 1))
                mm.then_inc(tT, 1)

            for g, j in T_ORDER:
                if g == "Tb":
                    Tb(j)
                elif g == "Td":
                    Td(j)
                elif g == "Te":
                    Te(j)
                elif g == "Tg":
                    Tg(j)
                elif g == "Th":
                    Th(j)
                elif g == "Ti":
                    Ti(j)
                elif g == "Tj":
                    Tj(j)
                elif g == "TB2":
                    e.wait_ge(stsx, 16 * MC)
                    wV(e, "Vc2", NT - 1)
                    for ic in range(MC):
                        for hc2 in range(2):
                            mm = e.matmul(psum_big[:, hc2, :], sxt[:, ic, 0:128],
                                          wog[:, ic, hc2 * 512:(hc2 + 1) * 512],
                                          start=(ic == 0), stop=(ic == MC - 1))
                    mm.then_inc(tT, 1)
                elif g == "TB3":
                    wV(e, "VB3a", 0)
                    for ic in range(MC):
                        for hc2 in range(2):
                            mm = e.matmul(psum_big[:, hc2, :], sxt[:, ic, 128:256],
                                          wog[:, ic, hc2 * 512:(hc2 + 1) * 512],
                                          start=(ic == 0), stop=(ic == MC - 1))
                    mm.then_inc(tT, 1)
                elif g == "Ca":
                    c0 = j * TILE
                    if j == 0:
                        wV(e, "VB3b", 0)
                        wV(e, "Vi", NT - 1)
                    else:
                        wV(e, "Cb", j - 1)
                    wV(e, "Cc1", j - 2)
                    for mjc in range(MC):
                        for hc2 in range(2):
                            mm = e.matmul(psum_qkv[:, hc2, :], qt_fm[:, mjc, c0:c0 + 128],
                                          weff[:, mjc, hc2 * 512:(hc2 + 1) * 512],
                                          start=(mjc == 0), stop=(mjc == MC - 1))
                        mm = e.matmul(psum_sum[:, (j & 1):(j & 1) + 1],
                                      qt_fm[:, mjc, c0:c0 + 128],
                                      wecol_bf[:, mjc:mjc + 1],
                                      start=(mjc == 0), stop=(mjc == MC - 1))
                    mm.then_inc(tT, 1)
                elif g == "Ch":
                    r = j & 1
                    e.wait_ge(stz3[r], 16 * (j // 2 + 1))
                    wV(e, "Ci", j - 1)
                    if j == 0:
                        wV(e, "VB3b", 0)
                    for kc in range(HC):
                        for oc in range(2):
                            mm = e.matmul(psum_big[:, oc, :], zf1[r][:, kc, :],
                                          swt[:, kc, oc * 512:(oc + 1) * 512],
                                          start=(kc == 0), stop=(kc == HC - 1))
                    mm.then_inc(tT, 1)

        @block.vector
        def _(e):
            AXX = AX.X
            e.memset(onesrow[0:1, :], 1.0)
            e.memset(cells[0:1, 14:15], 1.0)
            e.drain().then_inc(suV, 1)
            e.tensor_scalar(cells[0:1, 0:1], cells[0:1, 14:15], DECAY, None, ALU.mult)
            e.drain()
            for i in range(8):
                e.tensor_mul(cells[0:1, i + 1:i + 2], cells[0:1, i:i + 1],
                             cells[0:1, i:i + 1])
                e.drain()
            e.reciprocal(cells[0:1, 9:10], cells[0:1, 0:1])
            e.reciprocal(cells[0:1, 10:11], cells[0:1, 3:4])
            e.reciprocal(cells[0:1, 11:12], cells[0:1, PIDX:PIDX + 1])
            e.drain().then_inc(suV, 1)
            e.memset(qpat_in[0:1, :], 1.0)
            e.drain()
            e.tensor_scalar(qpat_in[0:1, 0:128:16], qpat_in[0:1, 0:128:16],
                            DECAY, None, ALU.mult)
            e.drain()
            e.tensor_tensor_scan(qpat_row[0:1, :], qpat_in[0:1, :], qpat_in[0:1, :],
                                 cells[0:1, 9:10], ALU.mult, ALU.bypass)
            e.drain().then_inc(suV, 1)
            e.reciprocal(kpat_row[0:1, :], qpat_row[0:1, :])
            e.drain()
            e.tensor_scalar(kpat_row[0:1, :], kpat_row[0:1, :],
                            cells[0:1, 9:10], None, ALU.mult)
            e.drain().then_inc(suV, 1)
            e.memset(d8in[0:1, :], 1.0)
            e.drain()
            e.tensor_scalar(d8in[0:1, :], d8in[0:1, :], cells[0:1, 3:4], None, ALU.mult)
            e.drain()
            e.tensor_tensor_scan(d8row[0:1, :], d8in[0:1, :], d8in[0:1, :],
                                 cells[0:1, 10:11], ALU.mult, ALU.bypass)
            e.drain()
            e.reciprocal(dk8row[0:1, :], d8row[0:1, :])
            e.drain().then_inc(suV, 1)
            e.memset(p8in[0:1, :], 1.0)
            e.drain()
            e.tensor_scalar(p8in[0:1, :], p8in[0:1, :], cells[0:1, PIDX:PIDX + 1],
                            None, ALU.mult)
            e.drain()
            e.tensor_tensor_scan(p_row[0:1, :], p8in[0:1, :], p8in[0:1, :],
                                 cells[0:1, 11:12], ALU.mult, ALU.bypass)
            e.drain()
            e.reciprocal(qr_row[0:1, :], p_row[0:1, :])
            e.drain().then_inc(suV, 1)
            e.wait_ge(swl, SWL_ALL)
            e.tensor_mul(t8[0:1, :], smalls_sb[0:1, 0:8], p_row[0:1, :])
            e.drain()
            e.reduce_sum(cells[0:1, 15:16], t8[0:1, :], axis=AXX)
            e.drain()
            e.tensor_mul(w9[0:1, 0:8], qr_row[0:1, :], smalls_sb[0:1, 8:16])
            e.drain()
            e.tensor_scalar(w9[0:1, 0:8], w9[0:1, 0:8], cells[0:1, 15:16], 1.0 / 16.0,
                            ALU.mult, ALU.mult)
            e.tensor_copy(w9[0:1, 8:9], cells[0:1, 15:16])
            e.drain().then_inc(suV, 1)
            e.tensor_copy(flat(wqk),
                          stg[:, SOFF + 8:SOFF + 12, :].rearrange("p a b -> p (a b)"))
            e.drain().then_inc(suV, 1)
            e.tensor_copy(flat(wv),
                          stg[:, SOFF + 12:SOFF + 14, :].rearrange("p a b -> p (a b)"))
            e.drain().then_inc(suV, 1)
            e.wait_ge(suT, 1)
            e.tensor_copy(qbase[:, :], psum_att[0][:, 0, 0:1])
            e.tensor_copy(kbase[:, :], psum_att[0][:, 0, 1:2])
            e.drain().then_inc(suV, 1)
            e.wait_ge(suT, 2)
            e.tensor_scalar(qscale[:, :], psum_att[0][:, 2, 0:NT], qbase[:, :],
                            None, ALU.mult)
            e.drain().then_inc(suV, 1)
            e.tensor_scalar(kscale[:, :], psum_att[0][:, 2, NT:2 * NT], kbase[:, :],
                            None, ALU.mult)
            e.drain().then_inc(suV, 1)
            e.wait_ge(suT, 3)
            e.tensor_copy(wcols[:, :], psum_att[0][:, 1, 0:9])
            e.drain().then_inc(suV, 1)
            e.wait_ge(suT, 4)
            e.tensor_copy(gate_col[:, :], psum_att[0][:, 0, 2:3])
            e.drain().then_inc(suV, 1)
            e.wait_ge(suS, 2)
            for kc in range(HC):
                e.reduce_sum(s1f32[:, kc:kc + 1], swt[:, kc, :], axis=AXX)
            e.drain()
            e.tensor_copy(s1bf[:, :], s1f32[:, :])
            e.drain().then_inc(suV, 1)

            def Va1(j):
                r = j & 3
                e.wait_ge(sxg[j // 4], 16)
                wS(e, "Sc", j - 4)
                e.reduce_sum(sumc[r], stream[:, j, :], axis=AXX)
                wS(e, "Sb", j)
                e.drain()
                e.tensor_scalar(mneg[r], sumc[r], -1.0 / H, None, ALU.mult)
                e.tensor_scalar(ve[r], sumsq[r], 1.0 / H, EPS, ALU.mult, ALU.add)
                e.drain()
                e.tensor_mul(msq[r], mneg[r], mneg[r])
                e.drain()
                e.tensor_sub(ve[r], ve[r], msq[r]).then_inc(tV, 1)

            def Va2(j):
                r = j & 3
                wS(e, "Sq1", j)
                e.reciprocal(inv2c[r], s0c[r])
                e.drain()
                e.tensor_scalar(invh[r], inv2c[r], 0.5, None, ALU.mult).then_inc(tV, 1)

            def Vc(j):
                r = j & 3
                wT(e, "Tb", j)
                e.scalar_tensor_tensor(stream[:, j, 0:512], psum_big[:, 0, :],
                                       invh[r], stream[:, j, 0:512],
                                       ALU.mult, ALU.add).then_inc(tV, 1)

            def Vc2(j):
                r = j & 3
                e.scalar_tensor_tensor(stream[:, j, 512:1024], psum_big[:, 1, :],
                                       invh[r], stream[:, j, 512:1024],
                                       ALU.mult, ALU.add).then_inc(tV, 1)

            def Vd1(j):
                r = j & 3
                wT(e, "Tb", j)
                e.tensor_scalar(tmpc[r], psum_sum[:, (j & 1):(j & 1) + 1],
                                invh[r], -1.0 / H, ALU.mult, ALU.mult)
                e.drain()
                e.scalar_tensor_tensor(mnegall[:, j:j + 1], sumc[r], -1.0 / H,
                                       tmpc[r], ALU.mult, ALU.add).then_inc(tV, 1)

            def Vw(j):
                r = j & 3
                wS(e, "Sd", j)
                e.tensor_scalar(ve[r], sumsq[r], 1.0 / H, EPS, ALU.mult, ALU.add)
                e.drain()
                e.tensor_mul(msq[r], mnegall[:, j:j + 1], mnegall[:, j:j + 1])
                e.drain()
                e.tensor_sub(ve[r], ve[r], msq[r]).then_inc(tV, 1)

            def Vd2(j):
                r = j & 3
                wS(e, "Sq2", j)
                e.reciprocal(inv2c[r], s0c[r])
                e.drain()
                e.tensor_scalar(qcol[r], qscale[:, j:j + 1], inv2c[r], None, ALU.mult)
                e.tensor_scalar(kcol[r], kscale[:, j:j + 1], inv2c[r],
                                None, ALU.mult).then_inc(tV, 1)

            def Vg(j):
                r = j & 1
                wT(e, "Tg", j)
                e.tensor_mul(mskT[r][:, :], psum_att[r][:, 0, :],
                             mask_sb[:, :]).then_inc(tV, 1)

            def Vh(j):
                r = j & 1
                wT(e, "Th", j)
                e.tensor_copy(flat(rb[r]),
                              psum_att[r][:, 1:3, :].rearrange("p a b -> p (a b)")
                              ).then_inc(tV, 1)

            def Vi(j):
                wT(e, "Tj", j)
                e.tensor_add(stream[:, j, :], stream[:, j, :],
                             flat(psum_qkv)).then_inc(tV, 1)

            def Vj(j):
                wT(e, "Ti", j)
                if j == 0:
                    e.tensor_copy(flat(ss_f32), flat(psum_ss)).then_inc(tV, 1)
                else:
                    e.tensor_add(flat(ss_f32), flat(ss_f32),
                                 flat(psum_ss)).then_inc(tV, 1)

            for g, j in V_ORDER:
                if g == "Va1":
                    Va1(j)
                elif g == "Va2":
                    Va2(j)
                elif g == "Vc":
                    Vc(j)
                elif g == "Vc2":
                    Vc2(j)
                elif g == "Vd1":
                    Vd1(j)
                elif g == "Vw":
                    Vw(j)
                elif g == "Vd2":
                    Vd2(j)
                elif g == "Vg":
                    Vg(j)
                elif g == "Vh":
                    Vh(j)
                elif g == "Vi":
                    Vi(j)
                elif g == "Vj":
                    Vj(j)
                elif g == "VBc":
                    cp = j
                    e.wait_ge(scg[cp % 2], 16 * (cp // 2 + 1))
                    if cp == 0:
                        e.tensor_scalar(flat(ss_ext), flat(m0t_sb),
                                        wcols[:, 8:9], None, ALU.mult)
                        e.drain()
                    e.scalar_tensor_tensor(flat(ss_ext), flat(gbuf[cp % 2]),
                                           wcols[:, cp:cp + 1], flat(ss_ext),
                                           ALU.mult, ALU.add)
                    e.drain()
                    if cp == NP - 1:
                        wT(e, "Th", NT - 1)
                        e.tensor_copy(flat(ssb[0]), flat(ss_ext)).then_inc(tV, 1)
                    else:
                        e.nop()
                        e.drain().then_inc(tV, 1)
                elif g == "VB3a":
                    wT(e, "TB2", 0)
                    e.tensor_copy(weff[:, 0, :], flat(psum_big)).then_inc(tV, 1)
                elif g == "VB3b":
                    wT(e, "TB3", 0)
                    e.tensor_copy(weff[:, 1, :], flat(psum_big))
                    e.drain()
                    for c in range(MC):
                        e.reduce_sum(wecol[:, c:c + 1], weff[:, c, :], axis=AXX)
                    e.drain()
                    e.tensor_copy(wecol_bf[:, :], wecol[:, :])
                    e.drain().then_inc(tV, 1)
                elif g == "Cb":
                    wT(e, "Ca", j)
                    e.drain()
                    e.tensor_add(stream[:, j, :], stream[:, j, :],
                                 flat(psum_qkv)).then_inc(tV, 1)
                elif g == "Cc1":
                    r = j & 3
                    wS(e, "Ce", j - 2)
                    wT(e, "Ca", j)
                    e.scalar_tensor_tensor(mneg[r], psum_sum[:, (j & 1):(j & 1) + 1],
                                           -1.0 / H, mnegall[:, j:j + 1],
                                           ALU.mult, ALU.add).then_inc(tV, 1)
                elif g == "Cw":
                    r = j & 3
                    wS(e, "Cd", j)
                    e.tensor_scalar(ve[r], sumsq[r], 1.0 / H, EPS, ALU.mult, ALU.add)
                    e.drain()
                    e.tensor_mul(msq[r], mneg[r], mneg[r])
                    e.drain()
                    e.tensor_sub(ve[r], ve[r], msq[r]).then_inc(tV, 1)
                elif g == "Cc2":
                    r = j & 3
                    wS(e, "Cq", j)
                    e.reciprocal(inv2c[r], s0c[r])
                    e.drain()
                    e.tensor_scalar(invh[r], inv2c[r], 0.5,
                                    None, ALU.mult).then_inc(tV, 1)
                elif g == "Ci":
                    r = j & 3
                    wT(e, "Ch", j)
                    e.scalar_tensor_tensor(stream[:, j, :], flat(psum_big),
                                           invh[r], stream[:, j, :],
                                           ALU.mult, ALU.add).then_inc(tV, 1)

        @block.scalar
        def _(e):
            e.wait_ge(swl, SWL_ALL)
            e.activation(cells[0:1, 13:14], smalls_sb[0:1, 16:17],
                         ACT.Sigmoid).then_inc(suS, 1)
            e.activation(flat(swt), stg[:, SOFF:SOFF + 8, :].rearrange("p a b -> p (a b)"),
                         ACT.Sign).then_inc(suS, 1)
            e.wait_ge(suV, SUV_ALL)
            for c in range(MC):
                aa = e.activation(wog[:, c, :], stg[:, SOFF + 14 + c, :], ACT.Copy,
                                  scale=gate_col[:, :])
            aa.then_inc(suS, 1)

            def Sb(j):
                r = j & 3
                e.wait_ge(sxg[j // 4], 16)
                if j >= 4:
                    e.wait_ge(stz2[j & 1], 16 * ((j - 4) // 2 + 1))
                    e.wait_ge(stz2h[j & 1], 16 * ((j - 4) // 2 + 1))
                wV(e, "Vd1", j - 4)
                e.activation(zb[r][:, :], stream[:, j, :], ACT.Square,
                             accum_out=sumsq[r]).then_inc(tS, 1)

            def Sq1(j):
                r = j & 3
                wV(e, "Va1", j)
                wV(e, "Va2", j - 4)
                e.activation(s0c[r], ve[r], ACT.Sqrt).then_inc(tS, 1)

            def Sa(j):
                r = j & 3
                wV(e, "Va1", j)
                e.activation(zb[r][:, :], stream[:, j, :], ACT.Identity,
                             bias=mneg[r], scale=1.0).then_inc(tS, 1)

            def Sd(j):
                r = j & 3
                wV(e, "Vc2", j)
                e.activation(zbc[j & 1][:, :], stream[:, j, :], ACT.Square,
                             accum_out=sumsq[r]).then_inc(tS, 1)

            def Sq2(j):
                r = j & 3
                wV(e, "Vw", j)
                wV(e, "Va2", j)
                e.activation(s0c[r], ve[r], ACT.Sqrt).then_inc(tS, 1)

            def Sc(j):
                r = j & 3
                wV(e, "Vc", j)
                e.wait_ge(stz1[j & 1], 16 * (j // 2 + 1))
                e.activation(zb[r][:, 0:512], stream[:, j, 0:512], ACT.Identity,
                             bias=mnegall[:, j:j + 1], scale=1.0).then_inc(tS, 1)

            def Sc2(j):
                r = j & 3
                wV(e, "Vc2", j)
                e.activation(zb[r][:, 512:1024], stream[:, j, 512:1024],
                             ACT.Identity,
                             bias=mnegall[:, j:j + 1], scale=1.0).then_inc(tS, 1)

            def Se(j):
                r = j & 1
                if j == 0:
                    e.memzero(flat(ssb[0])).then_inc(tS, 1)
                else:
                    wV(e, "Vj", j - 1)
                    wT(e, "Th", j - 2)
                    e.activation(flat(ssb[r]), flat(ss_f32), ACT.Copy,
                                 scale=1.0 / 16.0).then_inc(tS, 1)

            def Sf(j):
                r = j & 1
                wT(e, "Td", j)
                wV(e, "Vd2", j)
                if j >= 2:
                    e.wait_ge(stqk[r], 32 * ((j - 2) // 2 + 1))
                e.activation(qtb[r][:, :], psum_qkv[:, 0, 0:M], ACT.Copy,
                             scale=qcol[j & 3]).then_inc(tS, 1)

            def Sg(j):
                r = j & 1
                e.activation(khb[r][:, :], psum_qkv[:, 0, M:2 * M], ACT.Copy,
                             scale=kcol[j & 3]).then_inc(tS, 1)

            def Sh(j):
                r = j & 1
                wT(e, "Te", j)
                wT(e, "Ti", j - 2)
                e.activation(vvb[r][:, :], psum_qkv[:, 1, 0:M], ACT.Copy,
                             scale=inv2c[j & 3]).then_inc(tS, 1)

            for g, j in S_ORDER:
                if g == "Sb":
                    Sb(j)
                elif g == "Sc2":
                    Sc2(j)
                elif g == "Sq1":
                    Sq1(j)
                elif g == "Sa":
                    Sa(j)
                elif g == "Sd":
                    Sd(j)
                elif g == "Sq2":
                    Sq2(j)
                elif g == "Sc":
                    Sc(j)
                elif g == "Se":
                    Se(j)
                elif g == "Sf":
                    Sf(j)
                elif g == "Sg":
                    Sg(j)
                elif g == "Sh":
                    Sh(j)
                elif g == "Cd":
                    r = j & 1
                    r4 = j & 3
                    wV(e, "Cb", j)
                    if j >= 2:
                        e.wait_ge(stz3[r], 16 * ((j - 2) // 2 + 1))
                    wV(e, "Cc1", j - 2)
                    e.activation(zbc[r][:, :], stream[:, j, :], ACT.Square,
                                 accum_out=sumsq[r4]).then_inc(tS, 1)
                elif g == "Cq":
                    r = j & 3
                    wV(e, "Cw", j)
                    wV(e, "Cc2", j - 2)
                    e.activation(s0c[r], ve[r], ACT.Sqrt).then_inc(tS, 1)
                elif g == "Ce":
                    r = j & 1
                    wV(e, "Cc1", j)
                    e.drain()
                    e.activation(zbc[r][:, :], stream[:, j, :], ACT.Identity,
                                 bias=mneg[j & 3], scale=1.0).then_inc(tS, 1)

        @block.gpsimd
        def _(e):
            e.wait_ge(scc, 16)
            e.collective_compute(
                "AllGather", ALU.bypass,
                replica_groups=[list(range(NCORES))],
                ins=[cc_in.ap().opt()],
                outs=[cc_out.ap().opt()],
            ).then_inc(sccg, 1)

    es.close()
    return nc


def _host_layout(x, Wq, Wk, Wv, Wo, gate, M0, W, NT=32, NCORES=8):
    TPC = NT * TILE // B
    xs = []
    for c in range(NCORES):
        xc = np.ascontiguousarray(
            x[:, c * TPC:(c + 1) * TPC, :].transpose(1, 0, 2)).reshape(NT * TILE, H)
        xs.append(xc)
    wt = np.ascontiguousarray(W.T)
    wqkt = np.ascontiguousarray(np.concatenate([Wq.T, Wk.T], axis=1))
    wvt = np.ascontiguousarray(Wv.T)
    wot = np.ascontiguousarray(Wo.T)
    m0t = np.ascontiguousarray(M0.T)
    wtok = np.arange(TILE) // B
    mask = ((wtok[:, None] < wtok[None, :]).astype(np.float32) / np.float32(16.0))
    idb = np.eye(128).astype(ml_dtypes.bfloat16)
    gate_v = float(np.asarray(gate, np.float32).reshape(-1)[0])
    maps = []
    for c in range(NCORES):
        smalls = np.zeros((1, 128), np.float32)
        smalls[0, c] = 1.0
        smalls[0, 8:8 + c] = 1.0
        smalls[0, 16] = gate_v
        maps.append({
            "x": xs[c], "wt": wt, "wqkt": wqkt, "wvt": wvt, "wot": wot,
            "m0t": m0t, "mask": mask, "idb": idb, "smalls": smalls,
        })
    return maps


def kernel(L0_attn_x, L0_attn_Wq, L0_attn_Wk, L0_attn_Wv, L0_attn_Wo,
           L0_attn_gate, L0_attn_M0, L0_ffn_weight):
    NT, NCORES = 32, 8
    x = np.asarray(L0_attn_x, np.float32)
    maps = _host_layout(x, np.asarray(L0_attn_Wq, np.float32),
                        np.asarray(L0_attn_Wk, np.float32),
                        np.asarray(L0_attn_Wv, np.float32),
                        np.asarray(L0_attn_Wo, np.float32),
                        np.asarray(L0_attn_gate, np.float32),
                        np.asarray(L0_attn_M0, np.float32),
                        np.asarray(L0_ffn_weight, np.float32), NT, NCORES)
    nc = build_nc(NT, NCORES)
    res = run_bass_kernel_spmd(nc, maps, core_ids=list(range(NCORES)))
    Bv, T = x.shape[0], x.shape[1]
    TPC = T // NCORES
    out = np.empty((Bv, T, H), np.float32)
    for c in range(NCORES):
        yc = np.asarray(res.results[c]["y"]).reshape(TPC, B, H)
        out[:, c * TPC:(c + 1) * TPC, :] = yc.transpose(1, 0, 2)
    return out


# version 22
# speedup vs baseline: 1.0300x; 1.0228x over previous
import numpy as np
import ml_dtypes
import concourse.bass as bass
import concourse.mybir as mybir
from concourse.bass_utils import run_bass_kernel_spmd

F32 = mybir.dt.float32
BF16 = mybir.dt.bfloat16
AX = mybir.AxisListType
ALU = mybir.AluOpType
ACT = mybir.ActivationFunctionType

H = 1024
M = 256
B = 16
TILE = 128
HC = H // 128
MC = M // 128
DECAY = 0.99
EPS = 1e-5


def build_nc(NT=32, NCORES=8):
    NG = NT // 4
    nc = bass.Bass("TRN2", target_bir_lowering=False, debug=False,
                   dynamic_dma_scratch_size=512)

    x_d = nc.dram_tensor("x", [NT * TILE, H], F32, kind="ExternalInput").ap()
    wt_d = nc.dram_tensor("wt", [H, H], F32, kind="ExternalInput").ap()
    wqkt_d = nc.dram_tensor("wqkt", [H, 2 * M], F32, kind="ExternalInput").ap()
    wvt_d = nc.dram_tensor("wvt", [H, M], F32, kind="ExternalInput").ap()
    wot_d = nc.dram_tensor("wot", [M, H], F32, kind="ExternalInput").ap()
    m0t_d = nc.dram_tensor("m0t", [M, M], F32, kind="ExternalInput").ap()
    mask_d = nc.dram_tensor("mask", [128, 128], F32, kind="ExternalInput").ap()
    idb_d = nc.dram_tensor("idb", [128, 128], BF16, kind="ExternalInput").ap()
    smalls_d = nc.dram_tensor("smalls", [1, 128], F32, kind="ExternalInput").ap()
    y_d = nc.dram_tensor("y", [NT * TILE, H], F32, kind="ExternalOutput").ap()
    cc_in = nc.dram_tensor("cc_in", [M, M], F32)
    cc_out = nc.dram_tensor("cc_out", [NCORES * M, M], F32,
                            addr_space="Shared" if NCORES > 4 else "Local")

    import contextlib
    es = contextlib.ExitStack()
    sb = lambda name, shape, dt: es.enter_context(nc.sbuf_tensor(name, shape, dt))
    psum = lambda name, shape, dt=F32: es.enter_context(nc.psum_tensor(name, shape, dt))
    sem = lambda name: es.enter_context(nc.semaphore(name))

    stream = sb("stream", [128, NT, H], F32)
    if NT >= 32:
        stg, SOFF, stage_in_stream = stream, 16, True
    else:
        stg, SOFF = sb("wstage", [128, 16, H], F32), 0
        stage_in_stream = False
    qt_fm = sb("qt_fm", [128, MC, NT * TILE], BF16)
    swt = sb("swt", [128, HC, H], BF16)
    wqk = sb("wqk", [128, HC, 2 * M], BF16)
    wv = sb("wv", [128, HC, M], BF16)
    wog = sb("wog", [128, MC, H], BF16)
    weff = sb("weff", [128, MC, H], BF16)
    m0t_sb = sb("m0t_sb", [128, MC, M], F32)
    gbuf = [sb(f"gbuf{i}", [128, MC, M], F32) for i in range(2)]
    zb = [sb(f"zb{i}", [128, H], BF16) for i in range(4)]
    zbc = [sb(f"zbc{i}", [128, H], BF16) for i in range(2)]
    zf1 = [sb(f"zf1_{i}", [128, HC, 128], BF16) for i in range(2)]
    zf2 = [sb(f"zf2_{i}", [128, HC, 128], BF16) for i in range(2)]
    qtb = [sb(f"qtb{i}", [128, M], BF16) for i in range(2)]
    khb = [sb(f"khb{i}", [128, M], BF16) for i in range(2)]
    vvb = [sb(f"vvb{i}", [128, M], BF16) for i in range(2)]
    kf = [sb(f"kf{i}", [128, MC, 128], BF16) for i in range(2)]
    mskT = [sb(f"mskT{i}", [128, 128], BF16) for i in range(2)]
    ssb = [sb(f"ssb{i}", [128, MC, M], BF16) for i in range(2)]
    rb = [sb(f"rb{i}", [128, MC, 128], BF16) for i in range(2)]
    ss_f32 = sb("ss_f32", [128, MC, M], F32)
    ss_ext = sb("ss_ext", [128, MC, M], F32)
    sxt = sb("sxt", [128, MC, M], BF16)
    mask_sb = sb("mask_sb", [128, 128], F32)
    idb_sb = sb("idb_sb", [128, 128], BF16)
    smalls_sb = sb("smalls_sb", [1, 128], F32)
    gate_col = sb("gate_col", [128, 1], F32)
    qscale = sb("qscale", [128, NT], F32)
    kscale = sb("kscale", [128, NT], F32)
    wcols = sb("wcols", [128, 9], F32)
    qbase = sb("qbase", [128, 1], F32)
    kbase = sb("kbase", [128, 1], F32)
    onesrow = sb("onesrow", [1, 128], F32)
    cells = sb("cells", [1, 16], F32)
    qpat_in = sb("qpat_in", [1, 128], F32)
    qpat_row = sb("qpat_row", [1, 128], F32)
    kpat_row = sb("kpat_row", [1, 128], F32)
    d8in = sb("d8in", [1, NT], F32)
    d8row = sb("d8row", [1, NT], F32)
    dk8row = sb("dk8row", [1, NT], F32)
    p8in = sb("p8in", [1, 8], F32)
    p_row = sb("p_row", [1, 8], F32)
    qr_row = sb("qr_row", [1, 8], F32)
    t8 = sb("t8", [1, 8], F32)
    w9 = sb("w9", [1, 9], F32)
    statc = [sb(f"statc{i}", [128, 16], F32) for i in range(4)]
    s1f32 = sb("s1f32", [128, HC], F32)
    s1bf = sb("s1bf", [128, HC], BF16)
    mnegall = sb("mnegall", [128, NT], F32)
    wecol = sb("wecol", [128, MC], F32)
    wecol_bf = sb("wecol_bf", [128, MC], BF16)

    def _col(idx):
        return [statc[k][:, idx:idx + 1] for k in range(4)]
    sumc, sumsq, mneg, ve, msq, s0c = (_col(0), _col(1), _col(2), _col(3),
                                       _col(4), _col(5))
    r0c, tc_, invh, inv2c, qcol, kcol = (_col(6), _col(7), _col(8), _col(9),
                                         _col(10), _col(11))
    sump, tmpc = _col(12), _col(13)

    psum_ss = psum("psum_ss", [128, MC, M])
    psum_qkv = psum("psum_qkv", [128, 2, 512])
    psum_big = psum("psum_big", [128, 2, 512])
    psum_alt = psum("psum_alt", [128, 2, 512])
    psum_att = [psum_alt.ap()[:, i, 0:384].rearrange("p (a b) -> p a b", b=128)
                for i in range(2)]
    psum_sum = psum("psum_sum", [128, 2])
    flat = lambda t: t.ap().rearrange("p a b -> p (a b)")

    swl = sem("swl")
    sxg = [sem(f"sxg{g}") for g in range(NG)]
    sst = sem("sst")
    scc = sem("scc")
    sccg = sem("sccg")
    scg = [sem("scg0"), sem("scg1")]
    suT, suV, suS = sem("suT"), sem("suV"), sem("suS")
    stz1 = [sem("stz1a"), sem("stz1b")]
    stz2 = [sem("stz2a"), sem("stz2b")]
    stz2h = [sem("stz2ha"), sem("stz2hb")]
    stz3 = [sem("stz3a"), sem("stz3b")]
    stqk = [sem("stqka"), sem("stqkb")]
    stsx = sem("stsx")
    tT, tV, tS = sem("tT"), sem("tV"), sem("tS")

    SWL_ALL = 8 * 16
    SUV_ALL = 15
    SUS_ALL = 3
    NP = NCORES
    PIDX = 3 + int(np.log2(NT))
    assert 2 ** (PIDX - 3) == NT, "NT must be a power of two"

    def _ord(body, lo, hi, prologue=(), epilogue=()):
        out = list(prologue)
        for j in range(lo, hi):
            for g, dj in body:
                jj = j + dj
                if 0 <= jj < NT:
                    out.append((g, jj))
        out += list(epilogue)
        return out

    T_ORDER = _ord([("Tb", 0), ("Tg", -1), ("Th", -1),
                    ("Ti", -1), ("Tj", -1), ("Td", 0), ("Te", 0)],
                   -1, NT + 1)
    T_ORDER += [("TB2", 0), ("TB3", 0)]
    T_ORDER += _ord([("Ca", 0), ("Ch", -2)], 0, NT + 2)

    V_ORDER = _ord([("Vd1", 0), ("Vc", 0), ("Vc2", 0), ("Vg", -1), ("Vh", -1),
                    ("Vw", 0), ("Vi", -1), ("Vj", -1), ("Vd2", 0),
                    ("Va1", 2), ("Va2", 2)],
                   -2, NT + 1)
    for cp in range(NP):
        V_ORDER.append(("VBc", cp))
    V_ORDER += [("VB3a", 0), ("VB3b", 0)]
    V_ORDER += _ord([("Cb", 0), ("Cc1", 0), ("Cw", 0), ("Cc2", 0),
                     ("Ci", -2)],
                    0, NT + 2)

    S_ORDER = _ord([("Sf", -1), ("Sg", -1), ("Sh", -1), ("Se", -1),
                    ("Sc", 0), ("Sc2", 0), ("Sd", 0), ("Sq2", 0), ("Sb", 2),
                    ("Sq1", 2), ("Sa", 2)],
                   -2, NT + 1)
    S_ORDER += _ord([("Cd", 0), ("Ce", 0), ("Cq", 0)], 0, NT)
    assert len(T_ORDER) == len(set(T_ORDER))
    assert len(V_ORDER) == len(set(V_ORDER))
    assert len(S_ORDER) == len(set(S_ORDER))

    mkT = {g: i + 1 for i, g in enumerate(T_ORDER)}
    mkV = {g: i + 1 for i, g in enumerate(V_ORDER)}
    mkS = {g: i + 1 for i, g in enumerate(S_ORDER)}

    def wT(e, g, j=0):
        if (g, j) in mkT:
            e.wait_ge(tT, mkT[(g, j)])

    def wV(e, g, j=0):
        if (g, j) in mkV:
            e.wait_ge(tV, mkV[(g, j)])

    def wS(e, g, j=0):
        if (g, j) in mkS:
            e.wait_ge(tS, mkS[(g, j)])

    xv = x_d.rearrange("(j p) h -> p j h", p=128)
    yv = y_d.rearrange("(j p) h -> p j h", p=128)

    with nc.Block() as block:

        @block.sync
        def _(e):
            e.dma_start(out=stream[:, 0:4, :],
                        in_=xv[:, 0:4, :]).then_inc(sxg[0], 16)
            e.dma_start(out=mask_sb[:, :], in_=mask_d[:, :]).then_inc(swl, 16)
            e.dma_start(out=idb_sb[:, :], in_=idb_d[:, :]).then_inc(swl, 16)
            e.dma_start(out=smalls_sb[:, :], in_=smalls_d[:, :]).then_inc(swl, 16)
            e.dma_start(out=m0t_sb[:, :, :],
                        in_=m0t_d.rearrange("(c p) i -> p c i", p=128)).then_inc(swl, 16)
            e.dma_start(out=stg[:, SOFF:SOFF + 8, :],
                        in_=wt_d.rearrange("(c p) i -> p c i", p=128)).then_inc(swl, 16)
            e.dma_start(out=stg[:, SOFF + 8:SOFF + 12, :].rearrange("p a (u i) -> p (a u) i", u=2),
                        in_=wqkt_d.rearrange("(c p) i -> p c i", p=128)).then_inc(swl, 16)
            e.dma_start(out=stg[:, SOFF + 12:SOFF + 14, :].rearrange("p a (u i) -> p (a u) i", u=4),
                        in_=wvt_d.rearrange("(c p) i -> p c i", p=128)).then_inc(swl, 16)
            e.dma_start(out=stg[:, SOFF + 14:SOFF + 16, :],
                        in_=wot_d.rearrange("(c p) i -> p c i", p=128)).then_inc(swl, 16)
            for g in range(1, NG):
                if stage_in_stream and g == 4:
                    e.wait_ge(suS, SUS_ALL)
                    e.wait_ge(suV, 9)
                e.dma_start(out=stream[:, 4 * g:4 * g + 4, :],
                            in_=xv[:, 4 * g:4 * g + 4, :]).then_inc(sxg[g], 16)
            for j in (0, 1):
                wS(e, "Sa", j)
                e.dma_start_transpose(zf1[j & 1][:, :, :],
                                      zb[j & 3][:, :]).then_inc(stz1[j & 1], 16)
            for j in range(NT):
                r2 = j & 1
                wS(e, "Sc", j)
                wT(e, "Te", j - 2)
                e.dma_start_transpose(zf2[r2][:, 0:4, :],
                                      zb[j & 3][:, 0:512]).then_inc(stz2[r2], 16)
                wS(e, "Sc2", j)
                e.dma_start_transpose(zf2[r2][:, 4:8, :],
                                      zb[j & 3][:, 512:1024]).then_inc(stz2h[r2], 16)
                if j + 2 < NT:
                    wS(e, "Sa", j + 2)
                    wT(e, "Tb", j)
                    e.dma_start_transpose(zf1[j & 1][:, :, :],
                                          zb[(j + 2) & 3][:, :]).then_inc(stz1[j & 1], 16)
                wS(e, "Sf", j)
                wT(e, "Tg", j - 2)
                e.dma_start_transpose(
                    qt_fm[:, :, j * TILE:(j + 1) * TILE],
                    qtb[r2][:, :]).then_inc(stqk[r2], 16)
                wS(e, "Sg", j)
                e.dma_start_transpose(kf[r2][:, :, :],
                                      khb[r2][:, :]).then_inc(stqk[r2], 16)
            wV(e, "Vj", NT - 1)
            e.dma_start(out=cc_in.ap().rearrange("(c p) i -> p c i", p=128),
                        in_=ss_f32[:, :, :]).then_inc(scc, 16)
            e.wait_ge(sccg, 1)
            gview = cc_out.ap().rearrange("(g p) i -> p g i", p=128)
            for cp in range(NP):
                if cp >= 2:
                    wV(e, "VBc", cp - 2)
                e.dma_start(out=gbuf[cp % 2][:, :, :],
                            in_=gview[:, 2 * cp:2 * cp + 2, :]).then_inc(scg[cp % 2], 16)
            wV(e, "VBc", NP - 1)
            for jc in range(MC):
                e.dma_start_transpose(
                    sxt[:, :, jc * 128:(jc + 1) * 128],
                    ssb[0][:, jc, :]).then_inc(stsx, 16)
            for j in range(NT):
                r2 = j & 1
                wS(e, "Ce", j)
                if j < 2:
                    wT(e, "Tb", NT - 2 + j)
                else:
                    wT(e, "Ch", j - 2)
                e.dma_start_transpose(zf1[r2][:, :, :],
                                      zbc[r2][:, :]).then_inc(stz3[r2], 16)
            for g in range(NG):
                wV(e, "Ci", 4 * g + 3)
                e.dma_start(out=yv[:, 4 * g:4 * g + 4, :],
                            in_=stream[:, 4 * g:4 * g + 4, :]).then_inc(sst, 16)
            e.wait_ge(sst, 16 * NG)

        @block.tensor
        def _(e):
            e.wait_ge(suV, 4)
            e.matmul(psum_alt[:, 0, 0:1], qpat_row[0:1, :], cells[0:1, 14:15],
                     start=True, stop=True)
            e.matmul(psum_alt[:, 0, 1:2], kpat_row[0:1, :], cells[0:1, 14:15],
                     start=True, stop=True).then_inc(suT, 1)
            e.wait_ge(suV, 5)
            e.wait_ge(suV, 10)
            e.matmul(psum_alt[:, 0, 256:256 + NT], onesrow[0:1, :], d8row[0:1, :],
                     start=True, stop=True)
            e.matmul(psum_alt[:, 0, 256 + NT:256 + 2 * NT], onesrow[0:1, :], dk8row[0:1, :],
                     start=True, stop=True).then_inc(suT, 1)
            e.wait_ge(suV, 7)
            e.wait_ge(suV, 12)
            e.matmul(psum_alt[:, 0, 128:137], onesrow[0:1, :], w9[0:1, :],
                     start=True, stop=True).then_inc(suT, 1)
            e.wait_ge(suS, 1)
            e.wait_ge(suV, 13)
            e.matmul(psum_alt[:, 0, 2:3], onesrow[0:1, :], cells[0:1, 13:14],
                     start=True, stop=True).then_inc(suT, 1)
            e.wait_ge(swl, SWL_ALL)
            e.wait_ge(suS, SUS_ALL)
            e.wait_ge(suV, SUV_ALL)

            def Tb(j):
                r = j & 1
                c = j & 1
                e.wait_ge(stz1[r], 16 * (j // 2 + 1))
                wV(e, "Vc2", j - 1)
                wV(e, "Vd1", j - 2)
                for kc in range(HC):
                    for oc in range(2):
                        e.matmul(psum_big[:, oc, :], zf1[r][:, kc, :],
                                 swt[:, kc, oc * 512:(oc + 1) * 512],
                                 start=(kc == 0), stop=(kc == HC - 1))
                    mm = e.matmul(psum_sum[:, c:c + 1], zf1[r][:, kc, :],
                                  s1bf[:, kc:kc + 1],
                                  start=(kc == 0), stop=(kc == HC - 1))
                mm.then_inc(tT, 1)

            def Td(j):
                r = j & 1
                wV(e, "Vi", j - 1)
                e.wait_ge(stz2[r], 16 * (j // 2 + 1))
                for kc in range(4):
                    e.matmul(psum_qkv[:, 0, :], zf2[r][:, kc, :], wqk[:, kc, :],
                             start=(kc == 0), stop=False)
                e.wait_ge(stz2h[r], 16 * (j // 2 + 1))
                for kc in range(4, HC):
                    mm = e.matmul(psum_qkv[:, 0, :], zf2[r][:, kc, :], wqk[:, kc, :],
                                  start=False, stop=(kc == HC - 1))
                mm.then_inc(tT, 1)

            def Te(j):
                r = j & 1
                for kc in range(HC):
                    mm = e.matmul(psum_qkv[:, 1, 0:M], zf2[r][:, kc, :], wv[:, kc, :],
                                  start=(kc == 0), stop=(kc == HC - 1))
                mm.then_inc(tT, 1)

            def Tg(j):
                r = j & 1
                c0 = j * TILE
                e.wait_ge(stqk[r], 32 * (j // 2 + 1))
                wV(e, "Vh", j - 2)
                for c in range(MC):
                    mm = e.matmul(psum_alt[:, r, 0:128], kf[r][:, c, :],
                                  qt_fm[:, c, c0:c0 + 128],
                                  start=(c == 0), stop=(c == MC - 1))
                mm.then_inc(tT, 1)

            def Th(j):
                r = j & 1
                c0 = j * TILE
                wV(e, "Vg", j)
                wS(e, "Se", j)
                wV(e, "Vh", j - 2)
                for mi in range(MC):
                    for mj in range(MC):
                        e.matmul(psum_alt[:, r, 128 + mi * 128:256 + mi * 128],
                                 ssb[r][:, mj, mi * 128:(mi + 1) * 128],
                                 qt_fm[:, mj, c0:c0 + 128],
                                 start=(mi == 0 and mj == 0), stop=False)
                    mm = e.matmul(psum_alt[:, r, 128 + mi * 128:256 + mi * 128], vvb[r][:, mi * 128:(mi + 1) * 128],
                                  mskT[r][:, :], start=False, stop=(mi == MC - 1))
                mm.then_inc(tT, 1)

            def Ti(j):
                r = j & 1
                wS(e, "Sh", j)
                wV(e, "Vj", j - 1)
                for mk in range(MC):
                    mm = e.matmul(psum_ss[:, mk, :], khb[r][:, mk * 128:(mk + 1) * 128],
                                  vvb[r][:, :], start=(mk == 0), stop=(mk == MC - 1))
                mm.then_inc(tT, 1)

            def Tj(j):
                r = j & 1
                wV(e, "Vh", j)
                wS(e, "Sh", j)
                for kc in range(MC):
                    for oc in range(2):
                        mm = e.matmul(psum_qkv[:, oc, :], rb[r][:, kc, :],
                                      wog[:, kc, oc * 512:(oc + 1) * 512],
                                      start=(kc == 0), stop=(kc == MC - 1))
                mm.then_inc(tT, 1)

            for g, j in T_ORDER:
                if g == "Tb":
                    Tb(j)
                elif g == "Td":
                    Td(j)
                elif g == "Te":
                    Te(j)
                elif g == "Tg":
                    Tg(j)
                elif g == "Th":
                    Th(j)
                elif g == "Ti":
                    Ti(j)
                elif g == "Tj":
                    Tj(j)
                elif g == "TB2":
                    e.wait_ge(stsx, 16 * MC)
                    wV(e, "Vc2", NT - 1)
                    for ic in range(MC):
                        for hc2 in range(2):
                            mm = e.matmul(psum_big[:, hc2, :], sxt[:, ic, 0:128],
                                          wog[:, ic, hc2 * 512:(hc2 + 1) * 512],
                                          start=(ic == 0), stop=(ic == MC - 1))
                    mm.then_inc(tT, 1)
                elif g == "TB3":
                    wV(e, "VB3a", 0)
                    for ic in range(MC):
                        for hc2 in range(2):
                            mm = e.matmul(psum_big[:, hc2, :], sxt[:, ic, 128:256],
                                          wog[:, ic, hc2 * 512:(hc2 + 1) * 512],
                                          start=(ic == 0), stop=(ic == MC - 1))
                    mm.then_inc(tT, 1)
                elif g == "Ca":
                    c0 = j * TILE
                    if j == 0:
                        wV(e, "VB3b", 0)
                        wV(e, "Vi", NT - 1)
                    else:
                        wV(e, "Cb", j - 1)
                    wV(e, "Cc1", j - 2)
                    for mjc in range(MC):
                        for hc2 in range(2):
                            mm = e.matmul(psum_qkv[:, hc2, :], qt_fm[:, mjc, c0:c0 + 128],
                                          weff[:, mjc, hc2 * 512:(hc2 + 1) * 512],
                                          start=(mjc == 0), stop=(mjc == MC - 1))
                        mm = e.matmul(psum_sum[:, (j & 1):(j & 1) + 1],
                                      qt_fm[:, mjc, c0:c0 + 128],
                                      wecol_bf[:, mjc:mjc + 1],
                                      start=(mjc == 0), stop=(mjc == MC - 1))
                    mm.then_inc(tT, 1)
                elif g == "Ch":
                    r = j & 1
                    pout = psum_big if r == 0 else psum_alt
                    e.wait_ge(stz3[r], 16 * (j // 2 + 1))
                    wV(e, "Ci", j - 2)
                    if j == 0:
                        wV(e, "VB3b", 0)
                    elif j == 1:
                        wV(e, "Vh", NT - 1)
                        wV(e, "Vg", NT - 1)
                    for kc in range(HC):
                        for oc in range(2):
                            mm = e.matmul(pout.ap()[:, oc, :] if r == 0 else
                                          pout[:, oc, :],
                                          zf1[r][:, kc, :],
                                          swt[:, kc, oc * 512:(oc + 1) * 512],
                                          start=(kc == 0), stop=(kc == HC - 1))
                    mm.then_inc(tT, 1)

        @block.vector
        def _(e):
            AXX = AX.X
            e.memset(onesrow[0:1, :], 1.0)
            e.memset(cells[0:1, 14:15], 1.0)
            e.drain().then_inc(suV, 1)
            e.tensor_scalar(cells[0:1, 0:1], cells[0:1, 14:15], DECAY, None, ALU.mult)
            e.drain()
            for i in range(8):
                e.tensor_mul(cells[0:1, i + 1:i + 2], cells[0:1, i:i + 1],
                             cells[0:1, i:i + 1])
                e.drain()
            e.reciprocal(cells[0:1, 9:10], cells[0:1, 0:1])
            e.reciprocal(cells[0:1, 10:11], cells[0:1, 3:4])
            e.reciprocal(cells[0:1, 11:12], cells[0:1, PIDX:PIDX + 1])
            e.drain().then_inc(suV, 1)
            e.memset(qpat_in[0:1, :], 1.0)
            e.drain()
            e.tensor_scalar(qpat_in[0:1, 0:128:16], qpat_in[0:1, 0:128:16],
                            DECAY, None, ALU.mult)
            e.drain()
            e.tensor_tensor_scan(qpat_row[0:1, :], qpat_in[0:1, :], qpat_in[0:1, :],
                                 cells[0:1, 9:10], ALU.mult, ALU.bypass)
            e.drain().then_inc(suV, 1)
            e.reciprocal(kpat_row[0:1, :], qpat_row[0:1, :])
            e.drain()
            e.tensor_scalar(kpat_row[0:1, :], kpat_row[0:1, :],
                            cells[0:1, 9:10], None, ALU.mult)
            e.drain().then_inc(suV, 1)
            e.memset(d8in[0:1, :], 1.0)
            e.drain()
            e.tensor_scalar(d8in[0:1, :], d8in[0:1, :], cells[0:1, 3:4], None, ALU.mult)
            e.drain()
            e.tensor_tensor_scan(d8row[0:1, :], d8in[0:1, :], d8in[0:1, :],
                                 cells[0:1, 10:11], ALU.mult, ALU.bypass)
            e.drain()
            e.reciprocal(dk8row[0:1, :], d8row[0:1, :])
            e.drain().then_inc(suV, 1)
            e.memset(p8in[0:1, :], 1.0)
            e.drain()
            e.tensor_scalar(p8in[0:1, :], p8in[0:1, :], cells[0:1, PIDX:PIDX + 1],
                            None, ALU.mult)
            e.drain()
            e.tensor_tensor_scan(p_row[0:1, :], p8in[0:1, :], p8in[0:1, :],
                                 cells[0:1, 11:12], ALU.mult, ALU.bypass)
            e.drain()
            e.reciprocal(qr_row[0:1, :], p_row[0:1, :])
            e.drain().then_inc(suV, 1)
            e.wait_ge(swl, SWL_ALL)
            e.tensor_mul(t8[0:1, :], smalls_sb[0:1, 0:8], p_row[0:1, :])
            e.drain()
            e.reduce_sum(cells[0:1, 15:16], t8[0:1, :], axis=AXX)
            e.drain()
            e.tensor_mul(w9[0:1, 0:8], qr_row[0:1, :], smalls_sb[0:1, 8:16])
            e.drain()
            e.tensor_scalar(w9[0:1, 0:8], w9[0:1, 0:8], cells[0:1, 15:16], 1.0 / 16.0,
                            ALU.mult, ALU.mult)
            e.tensor_copy(w9[0:1, 8:9], cells[0:1, 15:16])
            e.drain().then_inc(suV, 1)
            e.tensor_copy(flat(wqk),
                          stg[:, SOFF + 8:SOFF + 12, :].rearrange("p a b -> p (a b)"))
            e.drain().then_inc(suV, 1)
            e.tensor_copy(flat(wv),
                          stg[:, SOFF + 12:SOFF + 14, :].rearrange("p a b -> p (a b)"))
            e.drain().then_inc(suV, 1)
            e.wait_ge(suT, 1)
            e.tensor_copy(qbase[:, :], psum_alt[:, 0, 0:1])
            e.tensor_copy(kbase[:, :], psum_alt[:, 0, 1:2])
            e.drain().then_inc(suV, 1)
            e.wait_ge(suT, 2)
            e.tensor_scalar(qscale[:, :], psum_alt[:, 0, 256:256 + NT], qbase[:, :],
                            None, ALU.mult)
            e.drain().then_inc(suV, 1)
            e.tensor_scalar(kscale[:, :], psum_alt[:, 0, 256 + NT:256 + 2 * NT], kbase[:, :],
                            None, ALU.mult)
            e.drain().then_inc(suV, 1)
            e.wait_ge(suT, 3)
            e.tensor_copy(wcols[:, :], psum_alt[:, 0, 128:137])
            e.drain().then_inc(suV, 1)
            e.wait_ge(suT, 4)
            e.tensor_copy(gate_col[:, :], psum_alt[:, 0, 2:3])
            e.drain().then_inc(suV, 1)
            e.wait_ge(suS, 2)
            for kc in range(HC):
                e.reduce_sum(s1f32[:, kc:kc + 1], swt[:, kc, :], axis=AXX)
            e.drain()
            e.tensor_copy(s1bf[:, :], s1f32[:, :])
            e.drain().then_inc(suV, 1)

            def Va1(j):
                r = j & 3
                e.wait_ge(sxg[j // 4], 16)
                wS(e, "Sc", j - 4)
                e.reduce_sum(sumc[r], stream[:, j, :], axis=AXX)
                wS(e, "Sb", j)
                e.drain()
                e.tensor_scalar(mneg[r], sumc[r], -1.0 / H, None, ALU.mult)
                e.tensor_scalar(ve[r], sumsq[r], 1.0 / H, EPS, ALU.mult, ALU.add)
                e.drain()
                e.tensor_mul(msq[r], mneg[r], mneg[r])
                e.drain()
                e.tensor_sub(ve[r], ve[r], msq[r]).then_inc(tV, 1)

            def Va2(j):
                r = j & 3
                wS(e, "Sq1", j)
                e.reciprocal(inv2c[r], s0c[r])
                e.drain()
                e.tensor_scalar(invh[r], inv2c[r], 0.5, None, ALU.mult).then_inc(tV, 1)

            def Vc(j):
                r = j & 3
                wT(e, "Tb", j)
                e.scalar_tensor_tensor(stream[:, j, 0:512], psum_big[:, 0, :],
                                       invh[r], stream[:, j, 0:512],
                                       ALU.mult, ALU.add).then_inc(tV, 1)

            def Vc2(j):
                r = j & 3
                e.scalar_tensor_tensor(stream[:, j, 512:1024], psum_big[:, 1, :],
                                       invh[r], stream[:, j, 512:1024],
                                       ALU.mult, ALU.add).then_inc(tV, 1)

            def Vd1(j):
                r = j & 3
                wT(e, "Tb", j)
                e.tensor_scalar(tmpc[r], psum_sum[:, (j & 1):(j & 1) + 1],
                                invh[r], -1.0 / H, ALU.mult, ALU.mult)
                e.drain()
                e.scalar_tensor_tensor(mnegall[:, j:j + 1], sumc[r], -1.0 / H,
                                       tmpc[r], ALU.mult, ALU.add).then_inc(tV, 1)

            def Vw(j):
                r = j & 3
                wS(e, "Sd", j)
                e.tensor_scalar(ve[r], sumsq[r], 1.0 / H, EPS, ALU.mult, ALU.add)
                e.drain()
                e.tensor_mul(msq[r], mnegall[:, j:j + 1], mnegall[:, j:j + 1])
                e.drain()
                e.tensor_sub(ve[r], ve[r], msq[r]).then_inc(tV, 1)

            def Vd2(j):
                r = j & 3
                wS(e, "Sq2", j)
                e.reciprocal(inv2c[r], s0c[r])
                e.drain()
                e.tensor_scalar(qcol[r], qscale[:, j:j + 1], inv2c[r], None, ALU.mult)
                e.tensor_scalar(kcol[r], kscale[:, j:j + 1], inv2c[r],
                                None, ALU.mult).then_inc(tV, 1)

            def Vg(j):
                r = j & 1
                wT(e, "Tg", j)
                e.tensor_mul(mskT[r][:, :], psum_alt[:, r, 0:128],
                             mask_sb[:, :]).then_inc(tV, 1)

            def Vh(j):
                r = j & 1
                wT(e, "Th", j)
                e.tensor_copy(flat(rb[r]),
                              psum_alt[:, r, 128:384]
                              ).then_inc(tV, 1)

            def Vi(j):
                wT(e, "Tj", j)
                e.tensor_add(stream[:, j, :], stream[:, j, :],
                             flat(psum_qkv)).then_inc(tV, 1)

            def Vj(j):
                wT(e, "Ti", j)
                if j == 0:
                    e.tensor_copy(flat(ss_f32), flat(psum_ss)).then_inc(tV, 1)
                else:
                    e.tensor_add(flat(ss_f32), flat(ss_f32),
                                 flat(psum_ss)).then_inc(tV, 1)

            for g, j in V_ORDER:
                if g == "Va1":
                    Va1(j)
                elif g == "Va2":
                    Va2(j)
                elif g == "Vc":
                    Vc(j)
                elif g == "Vc2":
                    Vc2(j)
                elif g == "Vd1":
                    Vd1(j)
                elif g == "Vw":
                    Vw(j)
                elif g == "Vd2":
                    Vd2(j)
                elif g == "Vg":
                    Vg(j)
                elif g == "Vh":
                    Vh(j)
                elif g == "Vi":
                    Vi(j)
                elif g == "Vj":
                    Vj(j)
                elif g == "VBc":
                    cp = j
                    e.wait_ge(scg[cp % 2], 16 * (cp // 2 + 1))
                    if cp == 0:
                        e.tensor_scalar(flat(ss_ext), flat(m0t_sb),
                                        wcols[:, 8:9], None, ALU.mult)
                        e.drain()
                    e.scalar_tensor_tensor(flat(ss_ext), flat(gbuf[cp % 2]),
                                           wcols[:, cp:cp + 1], flat(ss_ext),
                                           ALU.mult, ALU.add)
                    e.drain()
                    if cp == NP - 1:
                        wT(e, "Th", NT - 1)
                        e.tensor_copy(flat(ssb[0]), flat(ss_ext)).then_inc(tV, 1)
                    else:
                        e.nop()
                        e.drain().then_inc(tV, 1)
                elif g == "VB3a":
                    wT(e, "TB2", 0)
                    e.tensor_copy(weff[:, 0, :], flat(psum_big)).then_inc(tV, 1)
                elif g == "VB3b":
                    wT(e, "TB3", 0)
                    e.tensor_copy(weff[:, 1, :], flat(psum_big))
                    e.drain()
                    for c in range(MC):
                        e.reduce_sum(wecol[:, c:c + 1], weff[:, c, :], axis=AXX)
                    e.drain()
                    e.tensor_copy(wecol_bf[:, :], wecol[:, :])
                    e.drain().then_inc(tV, 1)
                elif g == "Cb":
                    wT(e, "Ca", j)
                    e.drain()
                    e.tensor_add(stream[:, j, :], stream[:, j, :],
                                 flat(psum_qkv)).then_inc(tV, 1)
                elif g == "Cc1":
                    r = j & 3
                    wS(e, "Ce", j - 2)
                    wT(e, "Ca", j)
                    e.scalar_tensor_tensor(mneg[r], psum_sum[:, (j & 1):(j & 1) + 1],
                                           -1.0 / H, mnegall[:, j:j + 1],
                                           ALU.mult, ALU.add).then_inc(tV, 1)
                elif g == "Cw":
                    r = j & 3
                    wS(e, "Cd", j)
                    e.tensor_scalar(ve[r], sumsq[r], 1.0 / H, EPS, ALU.mult, ALU.add)
                    e.drain()
                    e.tensor_mul(msq[r], mneg[r], mneg[r])
                    e.drain()
                    e.tensor_sub(ve[r], ve[r], msq[r]).then_inc(tV, 1)
                elif g == "Cc2":
                    r = j & 3
                    wS(e, "Cq", j)
                    e.reciprocal(inv2c[r], s0c[r])
                    e.drain()
                    e.tensor_scalar(invh[r], inv2c[r], 0.5,
                                    None, ALU.mult).then_inc(tV, 1)
                elif g == "Ci":
                    r = j & 3
                    pin = flat(psum_big) if (j & 1) == 0 else \
                        psum_alt.ap().rearrange("p a b -> p (a b)")
                    wT(e, "Ch", j)
                    e.scalar_tensor_tensor(stream[:, j, :], pin,
                                           invh[r], stream[:, j, :],
                                           ALU.mult, ALU.add).then_inc(tV, 1)

        @block.scalar
        def _(e):
            e.wait_ge(swl, SWL_ALL)
            e.activation(cells[0:1, 13:14], smalls_sb[0:1, 16:17],
                         ACT.Sigmoid).then_inc(suS, 1)
            e.activation(flat(swt), stg[:, SOFF:SOFF + 8, :].rearrange("p a b -> p (a b)"),
                         ACT.Sign).then_inc(suS, 1)
            e.wait_ge(suV, SUV_ALL)
            for c in range(MC):
                aa = e.activation(wog[:, c, :], stg[:, SOFF + 14 + c, :], ACT.Copy,
                                  scale=gate_col[:, :])
            aa.then_inc(suS, 1)

            def Sb(j):
                r = j & 3
                e.wait_ge(sxg[j // 4], 16)
                if j >= 4:
                    e.wait_ge(stz2[j & 1], 16 * ((j - 4) // 2 + 1))
                    e.wait_ge(stz2h[j & 1], 16 * ((j - 4) // 2 + 1))
                wV(e, "Vd1", j - 4)
                e.activation(zb[r][:, :], stream[:, j, :], ACT.Square,
                             accum_out=sumsq[r]).then_inc(tS, 1)

            def Sq1(j):
                r = j & 3
                wV(e, "Va1", j)
                wV(e, "Va2", j - 4)
                e.activation(s0c[r], ve[r], ACT.Sqrt).then_inc(tS, 1)

            def Sa(j):
                r = j & 3
                wV(e, "Va1", j)
                e.activation(zb[r][:, :], stream[:, j, :], ACT.Identity,
                             bias=mneg[r], scale=1.0).then_inc(tS, 1)

            def Sd(j):
                r = j & 3
                wV(e, "Vc2", j)
                e.activation(zbc[j & 1][:, :], stream[:, j, :], ACT.Square,
                             accum_out=sumsq[r]).then_inc(tS, 1)

            def Sq2(j):
                r = j & 3
                wV(e, "Vw", j)
                wV(e, "Va2", j)
                e.activation(s0c[r], ve[r], ACT.Sqrt).then_inc(tS, 1)

            def Sc(j):
                r = j & 3
                wV(e, "Vc", j)
                e.wait_ge(stz1[j & 1], 16 * (j // 2 + 1))
                e.activation(zb[r][:, 0:512], stream[:, j, 0:512], ACT.Identity,
                             bias=mnegall[:, j:j + 1], scale=1.0).then_inc(tS, 1)

            def Sc2(j):
                r = j & 3
                wV(e, "Vc2", j)
                e.activation(zb[r][:, 512:1024], stream[:, j, 512:1024],
                             ACT.Identity,
                             bias=mnegall[:, j:j + 1], scale=1.0).then_inc(tS, 1)

            def Se(j):
                r = j & 1
                if j == 0:
                    e.memzero(flat(ssb[0])).then_inc(tS, 1)
                else:
                    wV(e, "Vj", j - 1)
                    wT(e, "Th", j - 2)
                    e.activation(flat(ssb[r]), flat(ss_f32), ACT.Copy,
                                 scale=1.0 / 16.0).then_inc(tS, 1)

            def Sf(j):
                r = j & 1
                wT(e, "Td", j)
                wV(e, "Vd2", j)
                if j >= 2:
                    e.wait_ge(stqk[r], 32 * ((j - 2) // 2 + 1))
                e.activation(qtb[r][:, :], psum_qkv[:, 0, 0:M], ACT.Copy,
                             scale=qcol[j & 3]).then_inc(tS, 1)

            def Sg(j):
                r = j & 1
                e.activation(khb[r][:, :], psum_qkv[:, 0, M:2 * M], ACT.Copy,
                             scale=kcol[j & 3]).then_inc(tS, 1)

            def Sh(j):
                r = j & 1
                wT(e, "Te", j)
                wT(e, "Ti", j - 2)
                e.activation(vvb[r][:, :], psum_qkv[:, 1, 0:M], ACT.Copy,
                             scale=inv2c[j & 3]).then_inc(tS, 1)

            for g, j in S_ORDER:
                if g == "Sb":
                    Sb(j)
                elif g == "Sc2":
                    Sc2(j)
                elif g == "Sq1":
                    Sq1(j)
                elif g == "Sa":
                    Sa(j)
                elif g == "Sd":
                    Sd(j)
                elif g == "Sq2":
                    Sq2(j)
                elif g == "Sc":
                    Sc(j)
                elif g == "Se":
                    Se(j)
                elif g == "Sf":
                    Sf(j)
                elif g == "Sg":
                    Sg(j)
                elif g == "Sh":
                    Sh(j)
                elif g == "Cd":
                    r = j & 1
                    r4 = j & 3
                    wV(e, "Cb", j)
                    if j >= 2:
                        e.wait_ge(stz3[r], 16 * ((j - 2) // 2 + 1))
                    wV(e, "Cc1", j - 2)
                    e.activation(zbc[r][:, :], stream[:, j, :], ACT.Square,
                                 accum_out=sumsq[r4]).then_inc(tS, 1)
                elif g == "Cq":
                    r = j & 3
                    wV(e, "Cw", j)
                    wV(e, "Cc2", j - 2)
                    e.activation(s0c[r], ve[r], ACT.Sqrt).then_inc(tS, 1)
                elif g == "Ce":
                    r = j & 1
                    wV(e, "Cc1", j)
                    e.drain()
                    e.activation(zbc[r][:, :], stream[:, j, :], ACT.Identity,
                                 bias=mneg[j & 3], scale=1.0).then_inc(tS, 1)

        @block.gpsimd
        def _(e):
            e.wait_ge(scc, 16)
            e.collective_compute(
                "AllGather", ALU.bypass,
                replica_groups=[list(range(NCORES))],
                ins=[cc_in.ap().opt()],
                outs=[cc_out.ap().opt()],
            ).then_inc(sccg, 1)

    es.close()
    return nc


def _host_layout(x, Wq, Wk, Wv, Wo, gate, M0, W, NT=32, NCORES=8):
    TPC = NT * TILE // B
    xs = []
    for c in range(NCORES):
        xc = np.ascontiguousarray(
            x[:, c * TPC:(c + 1) * TPC, :].transpose(1, 0, 2)).reshape(NT * TILE, H)
        xs.append(xc)
    wt = np.ascontiguousarray(W.T)
    wqkt = np.ascontiguousarray(np.concatenate([Wq.T, Wk.T], axis=1))
    wvt = np.ascontiguousarray(Wv.T)
    wot = np.ascontiguousarray(Wo.T)
    m0t = np.ascontiguousarray(M0.T)
    wtok = np.arange(TILE) // B
    mask = ((wtok[:, None] < wtok[None, :]).astype(np.float32) / np.float32(16.0))
    idb = np.eye(128).astype(ml_dtypes.bfloat16)
    gate_v = float(np.asarray(gate, np.float32).reshape(-1)[0])
    maps = []
    for c in range(NCORES):
        smalls = np.zeros((1, 128), np.float32)
        smalls[0, c] = 1.0
        smalls[0, 8:8 + c] = 1.0
        smalls[0, 16] = gate_v
        maps.append({
            "x": xs[c], "wt": wt, "wqkt": wqkt, "wvt": wvt, "wot": wot,
            "m0t": m0t, "mask": mask, "idb": idb, "smalls": smalls,
        })
    return maps


def kernel(L0_attn_x, L0_attn_Wq, L0_attn_Wk, L0_attn_Wv, L0_attn_Wo,
           L0_attn_gate, L0_attn_M0, L0_ffn_weight):
    NT, NCORES = 32, 8
    x = np.asarray(L0_attn_x, np.float32)
    maps = _host_layout(x, np.asarray(L0_attn_Wq, np.float32),
                        np.asarray(L0_attn_Wk, np.float32),
                        np.asarray(L0_attn_Wv, np.float32),
                        np.asarray(L0_attn_Wo, np.float32),
                        np.asarray(L0_attn_gate, np.float32),
                        np.asarray(L0_attn_M0, np.float32),
                        np.asarray(L0_ffn_weight, np.float32), NT, NCORES)
    nc = build_nc(NT, NCORES)
    res = run_bass_kernel_spmd(nc, maps, core_ids=list(range(NCORES)))
    Bv, T = x.shape[0], x.shape[1]
    TPC = T // NCORES
    out = np.empty((Bv, T, H), np.float32)
    for c in range(NCORES):
        yc = np.asarray(res.results[c]["y"]).reshape(TPC, B, H)
        out[:, c * TPC:(c + 1) * TPC, :] = yc.transpose(1, 0, 2)
    return out
